# version 14
# baseline (speedup 1.0000x reference)
import math
import numpy as np
from contextlib import ExitStack

import concourse.bass as bass
import concourse.mybir as mybir
from concourse.bass_utils import run_bass_kernel_spmd

F32 = mybir.dt.float32
BF16 = mybir.dt.bfloat16
AF = mybir.ActivationFunctionType
ALU = mybir.AluOpType
AX = mybir.AxisListType.X

B, S, H = 4, 2048, 2048
NH, HN = 16, 128
SH = 1024
NORM = 1.0 / math.sqrt(HN)
EPS = 1e-5
NFC = H // 128

DEBUG_TAPS = False
BISECT_NO_INTERLEAVE = False
BISECT_NO_LAG = False


NLANES = 8


class Sched:
    ENGS = ("T", "V", "S", "G", "Y")

    def __init__(self):
        self.ops = {e: [] for e in self.ENGS}
        self.cnt = {e: 0 for e in self.ENGS}
        self.lane_cnt = {}
        self.dma_idx = {"G": 0, "Y": 0}

    def add(self, e, fn, deps=(), dma=False):
        ds = [d for d in deps if d is not None]
        if dma and e in ("G", "Y"):
            lane = f"{e}{self.dma_idx[e] % NLANES}"
            self.dma_idx[e] += 1
            prev = self.lane_cnt.get(lane, 0)
            if prev:
                ds.append((lane, prev))
            self.lane_cnt[lane] = prev + 16
            self.ops[e].append((fn, ds, (lane, 16)))
            return (lane, prev + 16)
        amt = 16 if dma else 1
        self.cnt[e] += amt
        tok = (e, self.cnt[e])
        self.ops[e].append((fn, ds, (e, amt)))
        return tok


def build_program():
    nc = bass.Bass("TRN2", target_bir_lowering=False)

    xT = nc.dram_tensor("xT", [B, H, S], F32, kind="ExternalInput").ap()
    xq = nc.dram_tensor("xq", [B, H, SH], F32, kind="ExternalInput").ap()
    xo = nc.dram_tensor("xo", [SH, H], F32, kind="ExternalInput").ap()
    wq_d = nc.dram_tensor("wq", [128, 4 * NFC * 128], F32, kind="ExternalInput").ap()
    wk_d = nc.dram_tensor("wk", [128, 4 * NFC * 128], F32, kind="ExternalInput").ap()
    wv_d = nc.dram_tensor("wv", [128, 4 * NFC * 128], F32, kind="ExternalInput").ap()
    wp_d = nc.dram_tensor("wp", [NFC, 128, H], F32, kind="ExternalInput").ap()
    wr_d = nc.dram_tensor("wr", [128, NFC * 8], F32, kind="ExternalInput").ap()
    ceye = nc.dram_tensor("ceye", [128, 128], F32, kind="ExternalInput").ap()
    cone = nc.dram_tensor("cone", [1, 128], F32, kind="ExternalInput").ap()
    cng2 = nc.dram_tensor("cng2", [128, 1], F32, kind="ExternalInput").ap()
    bp_d = nc.dram_tensor("bpj", [1, H], F32, kind="ExternalInput").ap()
    wgp = nc.dram_tensor("wgp", [9, 8, 128, NFC, 128], F32, kind="ExternalInput").ap()
    wup = nc.dram_tensor("wup", [9, 8, 128, NFC, 128], F32, kind="ExternalInput").ap()
    wdp = nc.dram_tensor("wdp", [9, 8, 128, H], F32, kind="ExternalInput").ap()
    outp = nc.dram_tensor("out", [SH, H], F32, kind="ExternalOutput").ap()
    taps = {}
    if DEBUG_TAPS:
        taps["t_ctx"] = nc.dram_tensor("t_ctx", [128, 16 * 1024], BF16, kind="ExternalOutput").ap()
        taps["t_y1"] = nc.dram_tensor("t_y1", [SH, H], F32, kind="ExternalOutput").ap()
        taps["t_comb"] = nc.dram_tensor("t_comb", [8, 128, 8], F32, kind="ExternalOutput").ap()
        taps["t_kfm"] = nc.dram_tensor("t_kfm", [128, 2048], BF16, kind="ExternalOutput").ap()
        taps["t_vtm"] = nc.dram_tensor("t_vtm", [128, 2048], BF16, kind="ExternalOutput").ap()
        taps["t_qfm"] = nc.dram_tensor("t_qfm", [128, 1024], BF16, kind="ExternalOutput").ap()
        taps["t_pnt"] = nc.dram_tensor("t_pnt", [128, 4096], BF16, kind="ExternalOutput").ap()

    sch = Sched()
    A = sch.add

    with ExitStack() as ctx:
        e_ = ctx.enter_context

        POOL_BYTES = 212736
        pool = e_(nc.sbuf_tensor("pool", [128, POOL_BYTES // 2], BF16))

        def carve(off_b, shape, dtype):
            esz = 2 if dtype == BF16 else 4
            n = int(np.prod(shape[1:])) * esz
            assert off_b % 2 == 0 and off_b + n <= POOL_BYTES, (off_b, n)
            v = pool.ap()[:, off_b // 2:(off_b + n) // 2]
            if dtype != BF16:
                v = v.bitcast(dtype)
            if len(shape) == 3:
                v = v.rearrange("p (a b) -> p a b", a=shape[1], b=shape[2])
            elif len(shape) == 4:
                v = v.rearrange("p (a b c) -> p a b c", a=shape[1], b=shape[2], c=shape[3])
            if shape[0] != 128:
                v = v[0:shape[0]]
            return v

        o_consts = 0
        o_A0 = 6912
        o_A1 = 23296
        o_C = 39680
        o_K = 88832
        o_V = 121600
        o_Q = 154368
        o_VS = 162560
        o_PT = 163584
        o_PP = 171776
        o_CX = 179968
        o_BPR = 23296 + 8192
        assert o_CX + 32768 <= POOL_BYTES

        ident = carve(o_consts, [128, 128], BF16)
        wrp_sb = carve(o_consts + 2304, [128, NFC, 8], BF16)
        ones_bf = carve(o_consts + 2560, [1, 128], BF16)
        combT = carve(o_consts + 2816, [8, 1024], BF16)
        so = [o_consts + 4864]

        def small(n=1, dtype=F32):
            esz = 4 if dtype == F32 else 2
            o = so[0]
            so[0] += max(((n * esz + 7) // 8) * 8, 8)
            assert so[0] <= 6912
            return carve(o, [128, n], dtype)

        negmax = [small(), small()]
        bias_t = [small(), small()]
        neg2 = small()
        ell = [small(), small()]
        ellA = [small(), small()]
        ellB = [small(), small()]
        rell = [small(), small()]
        ln_negsum = small(); ln_negmean = small(); ln_ssq = small()
        ln_vs = small(); ln_sd = small(); ln_rstd = small()
        r_m1 = small(); r_m2 = small(); r_dd = small(); r_w1 = small(); r_w2 = small()
        r_d1 = small(8); r_i1 = small(8); r_l2 = small(8); r_d2 = small(8)
        r_i2 = small(8); r_t1 = small(8); r_comb = small(8); r_combbf = small(8, BF16)
        comb_flat = small(64)

        xslab = [carve(o_A0, [128, NFC, 512], BF16), carve(o_A1, [128, NFC, 512], BF16)]
        wq_sb = carve(o_C, [128, 4, NFC, 128], BF16)
        wk_sb = carve(o_C + 16384, [128, 4, NFC, 128], BF16)
        wv_sb = carve(o_C + 32768, [128, 4, NFC, 128], BF16)
        k_fm = [[carve(o_K + pb * 16384 + h * 4096, [128, 2048], BF16) for h in range(4)]
                for pb in range(2)]
        v_tm = [[carve(o_V + pb * 16384 + h * 4096, [128, 16, 128], BF16) for h in range(4)]
                for pb in range(2)]
        q_fm = [carve(o_Q + h * 2048, [128, 1024], BF16) for h in range(4)]
        vscr = carve(o_VS, [128, 512], BF16)
        bpr = carve(o_BPR, [1, 2048], BF16)
        PnT = carve(o_PT, [128, 16, 256], BF16)
        Pp = [carve(o_PP, [128, 2048], BF16), carve(o_PP + 4096, [128, 2048], BF16)]
        ctx_sb = carve(o_CX, [128, 16, 1024], BF16)

        x_own = carve(o_A0, [128, 2048], F32)
        wgu_slab = [carve(o_A0 + i * 4096, [128, NFC, 128], BF16) for i in range(4)]
        wp_sb = [carve(o_K, [128, NFC, 512], BF16), carve(o_K + 16384, [128, NFC, 512], BF16)]
        y1_fm = carve(o_K, [128, NFC, 1024], BF16)
        yc = carve(o_V, [128, 2048], F32)
        y1bf = carve(o_V + 8192, [128, 2048], BF16)
        h_sb = [carve(o_V + 12288, [128, 8, 512], BF16), carve(o_V + 20480, [128, 8, 512], BF16)]
        cb_sb = [carve(o_V + 28672, [128, 512], F32), carve(o_V + 30720, [128, 512], F32)]
        tg_s = carve(o_Q, [128, 512], F32)
        th_s = carve(o_Q + 2048, [128, 512], F32)
        wd_sb = carve(o_CX, [128, 8, 2048], BF16)
        y1_tm = [carve(o_C + i * 8192, [128, 2048], F32) for i in range(6)] + \
                [carve(o_PT, [128, 2048], F32), carve(o_A1, [128, 2048], F32)]

        PSM = e_(nc.psum_tensor("PSM", [128, 2048], F32))
        PSA = e_(nc.psum_tensor("PSA", [128, 512], F32))
        PSB = e_(nc.psum_tensor("PSB", [128, 512], F32))
        PTA = e_(nc.psum_tensor("PTA", [128, 1024], BF16))
        PSC = e_(nc.psum_tensor("PSC", [128, 512], F32))

        sems = {e: e_(nc.semaphore("sem" + e)) for e in Sched.ENGS}
        for e in ("G", "Y"):
            for i in range(NLANES):
                sems[f"{e}{i}"] = e_(nc.semaphore(f"sem{e}{i}"))

        t_ident = A("G", lambda g: g.dma_start(out=ident, in_=ceye), dma=True)
        t_ones = A("G", lambda g: g.dma_start(out=ones_bf, in_=cone), dma=True)
        t_neg2 = A("G", lambda g: g.dma_start(out=neg2, in_=cng2), dma=True)

        t_wq = A("G", lambda g: g.dma_start(out=wq_sb, in_=wq_d.rearrange(
            "p (a b c) -> p a b c", a=4, b=NFC, c=128)), dma=True)
        t_wk = A("G", lambda g: g.dma_start(out=wk_sb, in_=wk_d.rearrange(
            "p (a b c) -> p a b c", a=4, b=NFC, c=128)), dma=True)
        t_wv = A("G", lambda g: g.dma_start(out=wv_sb, in_=wv_d.rearrange(
            "p (a b c) -> p a b c", a=4, b=NFC, c=128)), dma=True)
        t_wr = A("G", lambda g: g.dma_start(out=wrp_sb, in_=wr_d.rearrange(
            "p (a b) -> p a b", a=NFC, b=8)), dma=True)
        t_bpr = None

        st = {
            "slab_last": [None, None],
            "psA": None, "psB": None, "ptA": None, "psC": None,
            "sl": 0,
            "slab_tok": None, "mm_tok": None, "vev_tok": None, "vscr_war": None,
            "qkv_last_T": None,
        }
        kvq_done = [None] * B
        grp_scores_last = {}
        grp_av_last = {}

        MMC = 4

        def make_kv_steps(bp):
            pb = bp % 2
            steps = []
            kwar = lambda r: grp_scores_last.get((bp - 2, r))
            vwar = lambda r: grp_av_last.get((bp - 2, r))

            def slab_load(src, sidx):
                def go():
                    st["slab_tok"] = A("G", (lambda: lambda g: g.dma_start(
                        out=xslab[sidx], in_=src))(), deps=[st["slab_last"][sidx]], dma=True)
                return go

            def mm_chunk(kind, hd, sidx, c0, wsb):
                def go(kind=kind, hd=hd, sidx=sidx, c0=c0, wsb=wsb):
                    ps = PSB if kind == "v" else PSA

                    def f(t):
                        for fc in range(c0, c0 + MMC):
                            ins = t.matmul(ps[:, 0:512], wsb[:, hd, fc, :],
                                           xslab[sidx][:, fc, :],
                                           start=(fc == 0), stop=(fc == NFC - 1))
                        return ins
                    deps = [st["slab_tok"], st["psB"] if kind == "v" else st["psA"]]
                    if c0 == 0:
                        deps.append(t_wv if kind == "v" else t_wk)
                    tok = A("T", f, deps=deps)
                    st["qkv_last_T"] = tok
                    st["mm_tok"] = tok
                    if c0 + MMC == NFC:
                        if kind == "v":
                            ev = A("V", (lambda: lambda v: v.tensor_copy(
                                vscr, PSB[:, 0:512]))(), deps=[tok, st["vscr_war"]])
                            st["psB"] = ev
                            st["vev_tok"] = ev
                        else:
                            tg = go.tg
                            ev = A("V", (lambda hd=hd, tg=tg: lambda v: v.tensor_copy(
                                k_fm[pb][hd][:, tg * 512:(tg + 1) * 512], PSA[:, 0:512]))(),
                                deps=[tok, kwar(hd) if tg == 0 else None])
                            st["psA"] = ev
                            kvq_done[bp] = ev
                return go

            def v_transpose(hd, tg):
                def go(hd=hd, tg=tg):
                    def f(t):
                        for tb in range(4):
                            ins = t.transpose(PTA[:, tb * 128:(tb + 1) * 128],
                                              vscr[:, tb * 128:(tb + 1) * 128], ident)
                        return ins
                    tok = A("T", f, deps=[st["vev_tok"], t_ident, st["ptA"]])
                    st["qkv_last_T"] = tok
                    st["vscr_war"] = tok
                    ev = A("V", (lambda hd=hd, tg=tg: lambda v: v.tensor_copy(
                        v_tm[pb][hd][:, tg * 4:tg * 4 + 4, :],
                        PTA[:, 0:512].rearrange("p (a b) -> p a b", a=4)))(),
                        deps=[tok, vwar(hd) if tg == 0 else None])
                    st["ptA"] = ev
                    kvq_done[bp] = ev
                return go

            for tg in range(4):
                sidx = st["sl"] % 2
                src = xT[bp].rearrange("(fc p) s -> p fc s", p=128)[:, :, tg * 512:(tg + 1) * 512]
                steps.append(slab_load(src, sidx))
                for hd in range(4):
                    for c0 in range(0, NFC, MMC):
                        g_ = mm_chunk("k", hd, sidx, c0, wk_sb)
                        g_.tg = tg
                        steps.append(g_)
                    for c0 in range(0, NFC, MMC):
                        g_ = mm_chunk("v", hd, sidx, c0, wv_sb)
                        g_.tg = tg
                        steps.append(g_)
                    steps.append(v_transpose(hd, tg))
                steps.append((lambda sidx=sidx: (lambda: st["slab_last"].__setitem__(
                    sidx, st["qkv_last_T"])))())
                st["sl"] += 1
            return steps

        def emit_q_block(bp):
            for qg in range(2):
                sidx = st["sl"] % 2
                src = xq[bp].rearrange("(fc p) s -> p fc s", p=128)[:, :, qg * 512:(qg + 1) * 512]
                st["slab_tok"] = A("G", (lambda src=src, sidx=sidx: lambda g: g.dma_start(
                    out=xslab[sidx], in_=src))(), deps=[st["slab_last"][sidx]], dma=True)
                for hd in range(4):
                    def f_q(t, hd=hd, sidx=sidx):
                        for fc in range(NFC):
                            ins = t.matmul(PSA[:, 0:512], wq_sb[:, hd, fc, :],
                                           xslab[sidx][:, fc, :],
                                           start=(fc == 0), stop=(fc == NFC - 1))
                        return ins
                    t_qmm = A("T", f_q, deps=[st["slab_tok"], t_wq, st["psA"],
                                              grp_scores_last.get((bp - 1, hd)) if qg == 0 else None])
                    st["qkv_last_T"] = t_qmm
                    t_qev = A("V", (lambda hd=hd, qg=qg: lambda v: v.tensor_copy(
                        q_fm[hd][:, qg * 512:(qg + 1) * 512], PSA[:, 0:512]))(), deps=[t_qmm])
                    st["psA"] = t_qev
                    kvq_done[bp] = t_qev
                st["slab_last"][sidx] = t_qmm
                st["sl"] += 1

        def drain(steps, n):
            for _ in range(min(n, len(steps))):
                steps.pop(0)()
            return steps

        tok_exp = {}
        tok_expA = {}
        tok_norm = {}
        tok_trans = {}
        tok_evacB = {}
        tok_av = {}
        ctx_evac = {}
        tok_exp_prev = None
        gqt = 0

        def emit_trans_av(g):
            bp_g = g // 32
            r_g = (g // 8) % 4
            qt_g = g % 8
            pb_g = bp_g % 2
            islot = 4 * bp_g + r_g
            par = g % 2
            pr = qt_g % 2

            def f_trA(t, par=par):
                for c in range(8):
                    ins = t.transpose(PTA[:, c * 128:(c + 1) * 128],
                                      Pp[par][:, c * 128:(c + 1) * 128], ident)
                return ins
            t_trA = A("T", f_trA, deps=[tok_norm[g], st["ptA"], t_ident])
            t_evA = A("V", (lambda pr=pr: lambda v: v.tensor_copy(
                PnT[:, 0:8, pr * 128:(pr + 1) * 128],
                PTA[:, 0:1024].rearrange("p (a b) -> p a b", a=8)))(),
                deps=[t_trA, tok_av.get(g // 2 - 1)])
            st["ptA"] = t_evA

            def f_trB(t, par=par):
                for c in range(8):
                    ins = t.transpose(PTA[:, c * 128:(c + 1) * 128],
                                      Pp[par][:, (8 + c) * 128:(9 + c) * 128], ident)
                return ins
            t_trB = A("T", f_trB, deps=[tok_norm[g], t_evA, t_ident])
            tok_trans[g] = t_trB
            t_evB = A("V", (lambda pr=pr: lambda v: v.tensor_copy(
                PnT[:, 8:16, pr * 128:(pr + 1) * 128],
                PTA[:, 0:1024].rearrange("p (a b) -> p a b", a=8)))(),
                deps=[t_trB, tok_av.get(g // 2 - 1)])
            st["ptA"] = t_evB
            tok_evacB[g] = t_evB

            if qt_g % 2 == 1:
                pair = g // 2

                def f_av(t, hd=r_g, pb=pb_g):
                    for kc in range(16):
                        ins = t.matmul(PSC[:, 0:256], v_tm[pb][hd][:, kc, :],
                                       PnT[:, kc, :], start=(kc == 0), stop=(kc == 15))
                    return ins
                t_avmm = A("T", f_av, deps=[tok_evacB[g - 1], t_evB, st["psC"]])
                tok_av[pair] = t_avmm
                grp_av_last[(bp_g, r_g)] = t_avmm
                pq = qt_g // 2
                t_cxe = A("V", (lambda islot=islot, pq=pq: lambda v: v.tensor_copy(
                    ctx_sb[:, islot, pq * 256:(pq + 1) * 256], PSC[:, 0:256]))(),
                    deps=[t_avmm])
                ctx_evac[pair] = t_cxe
                st["psC"] = t_cxe

        pend = make_kv_steps(0)
        while pend:
            pend = drain(pend, 1000)
        emit_q_block(0)
        if DEBUG_TAPS:
            A("Y", lambda y: y.dma_start(out=taps["t_kfm"], in_=k_fm[0][0]),
              deps=[kvq_done[0]], dma=True)
            A("Y", lambda y: y.dma_start(
                out=taps["t_vtm"], in_=v_tm[0][0].rearrange("p a b -> p (a b)")),
              deps=[kvq_done[0]], dma=True)
            A("Y", lambda y: y.dma_start(out=taps["t_qfm"], in_=q_fm[0]),
              deps=[kvq_done[0]], dma=True)

        for bp in range(B):
            pb = bp % 2
            nxt = make_kv_steps(bp + 1) if bp + 1 < B else []
            if BISECT_NO_INTERLEAVE:
                while nxt:
                    nxt = drain(nxt, 1000)
            drain_calls = [32]
            my_kvq = kvq_done[bp]

            for r in range(4):
                hd = r
                islot = 4 * bp + r
                for qt in range(8):
                    par = gqt % 2

                    def f_scA(t, hd=hd, qt=qt, pb=pb):
                        for fb in range(2):
                            ins = t.matmul(PSM[:, fb * 512:(fb + 1) * 512],
                                           q_fm[hd][:, qt * 128:(qt + 1) * 128],
                                           k_fm[pb][hd][:, fb * 512:(fb + 1) * 512],
                                           start=True, stop=True)
                        return ins
                    t_scA = A("T", f_scA, deps=[tok_expA.get(gqt - 1), my_kvq])

                    def f_scB(t, hd=hd, qt=qt, pb=pb):
                        for fb in range(2, 4):
                            ins = t.matmul(PSM[:, fb * 512:(fb + 1) * 512],
                                           q_fm[hd][:, qt * 128:(qt + 1) * 128],
                                           k_fm[pb][hd][:, fb * 512:(fb + 1) * 512],
                                           start=True, stop=True)
                        return ins
                    t_scB = A("T", f_scB, deps=[tok_exp_prev, my_kvq])
                    grp_scores_last[(bp, r)] = t_scB
                    t_mx = A("V", (lambda par=par: lambda v: v.reduce_max(
                        negmax[par], PSM[:, 0:256], axis=AX, negate=True))(),
                        deps=[t_scA, tok_exp.get(gqt - 2)])
                    t_bi = A("V", (lambda par=par: lambda v: v.tensor_scalar(
                        bias_t[par], negmax[par], NORM, -2.0, ALU.mult, ALU.add))(),
                        deps=[t_mx])
                    t_exA = A("S", (lambda par=par: lambda s: s.activation(
                        Pp[par][:, 0:1024], PSM[:, 0:1024], AF.Exp, bias=bias_t[par],
                        scale=NORM, accum_out=ellA[par]))(),
                        deps=[t_bi, tok_trans.get(gqt - 2), tok_norm.get(gqt - 2)])
                    tok_expA[gqt] = t_exA
                    t_exB = A("S", (lambda par=par: lambda s: s.activation(
                        Pp[par][:, 1024:2048], PSM[:, 1024:2048], AF.Exp,
                        bias=bias_t[par], scale=NORM, accum_out=ellB[par]))(),
                        deps=[t_scB])
                    tok_exp[gqt] = t_exB
                    tok_exp_prev = t_exB

                    nxt = drain(nxt, -(-len(nxt) // max(1, drain_calls[0])))
                    drain_calls[0] -= 1

                    if gqt >= 1:
                        emit_trans_av(gqt - 1)

                    t_el = A("V", (lambda par=par: lambda v: v.tensor_add(
                        ell[par], ellA[par], ellB[par]))(), deps=[t_exB])
                    t_rc = A("V", (lambda par=par: lambda v: v.reciprocal(
                        rell[par], ell[par]))(), deps=[t_el])
                    t_no = A("V", (lambda par=par: lambda v: v.tensor_scalar(
                        Pp[par], Pp[par], rell[par], None, ALU.mult))(), deps=[t_rc])
                    tok_norm[gqt] = t_no
                    gqt += 1

            if bp == B - 1 and not BISECT_NO_LAG:
                emit_trans_av(gqt - 1)
            while nxt:
                nxt = drain(nxt, 1000)
            if bp + 1 < B:
                emit_q_block(bp + 1)


        attn_done_V = ctx_evac[gqt // 2 - 1]
        attn_done_S = tok_exp[gqt - 1]
        attn_done_T = tok_av[gqt // 2 - 1]
        qkv_last_T = st["qkv_last_T"]
        last_scores_T = grp_scores_last[(B - 1, 3)]

        if DEBUG_TAPS:
            A("Y", lambda y: y.dma_start(out=taps["t_ctx"],
                                         in_=carve(o_CX, [128, 16 * 1024], BF16)),
              deps=[attn_done_V], dma=True)

        t_bpr = A("G", lambda g: g.dma_start(out=bpr, in_=bp_d),
                  deps=[qkv_last_T], dma=True)
        wp_buf_last = [grp_scores_last[(B - 2, 3)], grp_scores_last[(B - 1, 3)]]
        rs = {"psA": st["psA"], "psB": st["psB"], "ptA": st["ptA"], "psC": st["psC"],
              "psm": None,
              "y1bf_last": None, "sq_prev": None, "yc_war": None, "routerV": None,
              "combbf_last": None, "combT_done": None,
              "gu_buf_last": [None, None],
              "th1_prev": None, "h_last": [None, None], "h_tok": [None, None],
              "wd_reader": None, "wd_tok": None, "cb": [None, None]}
        y1_tok = [None] * 8
        y1fm_tok = [None] * 8
        y1acc = {}
        combT_evac = [None] * 8
        xb_tok = [None] * 8
        proj_evt = {}
        psm_slice_free = [attn_done_S] * 4
        xown_last_use = None
        proj_last_T = None

        def ln_stats(src_tile, deps_v, deps_s):
            t_ns = A("V", lambda v: v.reduce_sum(
                ln_negsum, src_tile, axis=AX, negate=True), deps=deps_v)
            t_sq = A("S", lambda s: s.activation(
                y1bf, src_tile, AF.Square, accum_out=ln_ssq),
                deps=deps_s + [rs["y1bf_last"], rs.get("ssq_reader")])
            rs["sq_prev"] = t_sq
            t_nm = A("V", lambda v: v.tensor_scalar(
                ln_negmean, ln_negsum, 1.0 / H, None, ALU.mult), deps=[t_ns])
            t_vs = A("V", lambda v: v.tensor_scalar(
                ln_vs, ln_ssq, 1.0 / H, EPS, ALU.mult, ALU.add), deps=[t_sq, t_nm])
            rs["ssq_reader"] = t_vs
            t_m2 = A("V", lambda v: v.tensor_tensor(
                ln_sd, ln_negmean, ln_negmean, ALU.mult), deps=[t_vs])
            t_va = A("V", lambda v: v.tensor_sub(ln_vs, ln_vs, ln_sd), deps=[t_m2])
            t_sd = A("S", lambda s: s.activation(ln_sd, ln_vs, AF.Sqrt), deps=[t_va])
            t_rs_ = A("V", lambda v: v.reciprocal(ln_rstd, ln_sd), deps=[t_sd])
            t_sh = A("V", lambda v: v.tensor_tensor(
                ln_negmean, ln_negmean, ln_rstd, ALU.mult), deps=[t_rs_])
            return t_sh

        def emit_ln1_chain(qt):
            t_sh = ln_stats(y1_tm[qt], [proj_evt[(qt, 3)], rs["yc_war"]],
                            [proj_evt[(qt, 3)]])
            t_y1 = A("V", (lambda qt=qt: lambda v: v.tensor_scalar(
                y1_tm[qt], y1_tm[qt], ln_rstd, ln_negmean, ALU.mult, ALU.add))(),
                deps=[t_sh])
            y1_tok[qt] = t_y1
            y1acc[qt] = t_y1
            rs["yc_war"] = t_y1

        def emit_ln1(qt):
            t_yb = A("V", (lambda qt=qt: lambda v: v.tensor_copy(
                y1bf, y1_tm[qt]))(), deps=[y1_tok[qt], rs["y1bf_last"]])

            def f_t1(t):
                for c in range(8):
                    ins = t.transpose(PTA[:, c * 128:(c + 1) * 128],
                                      y1bf[:, c * 128:(c + 1) * 128], ident)
                return ins
            t_t1 = A("T", f_t1, deps=[t_yb, rs["ptA"]])
            t_e1 = A("S", (lambda qt=qt: lambda s: s.copy(
                y1_fm[:, 0:8, qt * 128:(qt + 1) * 128],
                PTA[:, 0:1024].rearrange("p (a b) -> p a b", a=8)))(),
                deps=[t_t1, proj_last_T if qt == 0 else None])
            rs["ptA"] = t_e1

            def f_t2(t):
                for c in range(8):
                    ins = t.transpose(PTA[:, c * 128:(c + 1) * 128],
                                      y1bf[:, (8 + c) * 128:(9 + c) * 128], ident)
                return ins
            t_t2 = A("T", f_t2, deps=[t_yb, rs["ptA"]])
            rs["y1bf_last"] = t_t2
            t_e2 = A("S", (lambda qt=qt: lambda s: s.copy(
                y1_fm[:, 8:16, qt * 128:(qt + 1) * 128],
                PTA[:, 0:1024].rearrange("p (a b) -> p a b", a=8)))(), deps=[t_t2])
            rs["ptA"] = t_e2
            y1fm_tok[qt] = t_e2
            if DEBUG_TAPS:
                A("Y", (lambda qt=qt: lambda y: y.dma_start(
                    out=taps["t_y1"][qt * 128:(qt + 1) * 128, :], in_=y1_tm[qt]))(),
                  deps=[t_y1], dma=True)

        def emit_router(qt):
            def f_lg(t, qt=qt):
                for fc in range(NFC):
                    ins = t.matmul(PSC[:, 0:8], y1_fm[:, fc, qt * 128:(qt + 1) * 128],
                                   wrp_sb[:, fc, :], start=(fc == 0), stop=(fc == NFC - 1))
                return ins
            t_lg = A("T", f_lg, deps=[y1fm_tok[qt], t_wr, rs["psC"]])
            t1 = A("V", lambda v: v.reduce_max(r_m1, PSC[:, 0:8], axis=AX),
                   deps=[t_lg, rs["routerV"]])
            t2 = A("V", lambda v: v.tensor_scalar(r_d1, PSC[:, 0:8], r_m1, None, ALU.subtract),
                   deps=[t1])
            t3 = A("V", lambda v: v.tensor_scalar(r_i1, r_d1, 0.0, None, ALU.is_ge), deps=[t2])
            t4 = A("V", lambda v: v.scalar_tensor_tensor(r_l2, r_i1, -1e9, PSC[:, 0:8],
                                                         ALU.mult, ALU.add), deps=[t3])
            t5 = A("V", lambda v: v.reduce_max(r_m2, r_l2, axis=AX), deps=[t4])
            t6 = A("V", lambda v: v.tensor_scalar(r_d2, r_l2, r_m2, None, ALU.subtract), deps=[t5])
            t7 = A("V", lambda v: v.tensor_scalar(r_i2, r_d2, 0.0, None, ALU.is_ge), deps=[t6])
            t8 = A("V", lambda v: v.tensor_sub(r_dd, r_m2, r_m1), deps=[t7])
            t9 = A("S", lambda s: s.activation(r_w1, r_dd, AF.Sigmoid, scale=-1.0), deps=[t8])
            t10 = A("V", lambda v: v.tensor_scalar(r_w2, r_w1, -1.0, 1.0, ALU.mult, ALU.add),
                    deps=[t9])
            t11 = A("V", lambda v: v.tensor_scalar(r_t1, r_i1, r_w1, None, ALU.mult), deps=[t10])
            t12 = A("V", lambda v: v.scalar_tensor_tensor(r_comb, r_i2, r_w2, r_t1,
                                                          ALU.mult, ALU.add), deps=[t11])
            rs["psC"] = t12
            t13 = A("V", (lambda qt=qt: lambda v: v.tensor_copy(
                comb_flat[:, qt * 8:(qt + 1) * 8], r_comb))(), deps=[t12])
            rs["routerV"] = t13
            combT_evac[qt] = t13
            if DEBUG_TAPS:
                A("Y", (lambda qt=qt: lambda y: y.dma_start(
                    out=taps["t_comb"][qt], in_=r_comb))(), deps=[t12], dma=True)

        for qp in range(4):
            bidx = qp % 2
            t_wp = A("G", (lambda qp=qp, bidx=bidx: lambda g: g.dma_start(
                out=wp_sb[bidx], in_=wp_d[:, :, qp * 512:(qp + 1) * 512].rearrange(
                    "a p s -> p a s")))(), deps=[wp_buf_last[bidx]], dma=True)
            t_pj_last = None
            for qt in range(8):
                if qp == 0:
                    t_xo = A("Y", (lambda qt=qt: lambda y: y.dma_start(
                        out=y1_tm[qt], in_=xo[qt * 128:(qt + 1) * 128, :]))(),
                        deps=[qkv_last_T if qt == 0 else None,
                              attn_done_T if qt == 0 else None,
                              attn_done_S if qt == 0 else None], dma=True)
                    xb_tok[qt] = t_xo
                    xown_last_use = t_xo
                sli = qt % 4

                def f_pj(t, qt=qt, bidx=bidx, sli=sli, qp=qp):
                    for i in range(16):
                        t.matmul(PSM[:, sli * 512:(sli + 1) * 512],
                                 ctx_sb[:, i, qt * 128:(qt + 1) * 128],
                                 wp_sb[bidx][:, i, :],
                                 start=(i == 0), stop=False)
                    return t.matmul(PSM[:, sli * 512:(sli + 1) * 512], ones_bf,
                                    bpr[:, qp * 512:(qp + 1) * 512],
                                    start=False, stop=True)
                t_pj = A("T", f_pj, deps=[t_wp, attn_done_V, psm_slice_free[sli],
                                          t_ones, t_bpr])
                t_pj_last = t_pj
                proj_last_T = t_pj
                t_ev = A("V", (lambda qt=qt, qp=qp, sli=sli: lambda v: v.tensor_tensor(
                    y1_tm[qt][:, qp * 512:(qp + 1) * 512], PSM[:, sli * 512:(sli + 1) * 512],
                    y1_tm[qt][:, qp * 512:(qp + 1) * 512], ALU.add))(),
                    deps=[t_pj, xb_tok[qt]])
                proj_evt[(qt, qp)] = t_ev
                psm_slice_free[sli] = t_ev
                if qp == 3:
                    emit_ln1_chain(qt)
            wp_buf_last[bidx] = t_pj_last


        y1fm_grp = {0: lambda: y1fm_tok[3], 1: lambda: y1fm_tok[7]}

        def emit_gu_block(ei, grps, load_wd, its=tuple(range(8))):
            if load_wd:
                rs["_wd_pend"] = ei
                rs["_wd_cnt"] = 0
            for it in its:
                gb = it % 2
                t_wg = A("G", (lambda ei=ei, it=it, gb=gb: lambda g: g.dma_start(
                    out=wgu_slab[gb], in_=wgp[ei, it]))(), deps=[rs["gu_buf_last"][gb]],
                    dma=True)
                t_wu = A("G", (lambda ei=ei, it=it, gb=gb: lambda g: g.dma_start(
                    out=wgu_slab[2 + gb], in_=wup[ei, it]))(), deps=[rs["gu_buf_last"][gb]],
                    dma=True)
                if rs.get("_wd_pend") is not None:
                    rs["_wd_cnt"] += 1
                    cnt = rs["_wd_cnt"]
                    wei = rs["_wd_pend"]
                    if cnt <= 8:
                        rs["wd_tok"] = A("G", (lambda wei=wei, ic=cnt - 1: lambda g:
                                              g.dma_start(out=wd_sb[:, ic, :],
                                                          in_=wdp[wei, ic]))(),
                            deps=[rs["wd_reader"] if cnt == 1 else None,
                                  proj_last_T if (wei == 0 and cnt == 1) else None],
                            dma=True)
                    if cnt == 8 or (it == its[-1] and ei != 0 and cnt < 8):
                        while rs["_wd_cnt"] < 8:
                            rs["_wd_cnt"] += 1
                            ic = rs["_wd_cnt"] - 1
                            rs["wd_tok"] = A("G", (lambda wei=wei, ic=ic: lambda g:
                                                  g.dma_start(out=wd_sb[:, ic, :],
                                                              in_=wdp[wei, ic]))(),
                                deps=[], dma=True)
                        rs["_wd_pend"] = None
                last_u = None
                for grp in grps:
                    def f_g(t, gb=gb, grp=grp):
                        for fc in range(NFC):
                            ins = t.matmul(PSA[:, 0:512], wgu_slab[gb][:, fc, :],
                                           y1_fm[:, fc, grp * 512:(grp + 1) * 512],
                                           start=(fc == 0), stop=(fc == NFC - 1))
                        return ins
                    t_gmm = A("T", f_g, deps=[t_wg, y1fm_grp[grp](), rs["psA"]])

                    def f_u(t, gb=gb, grp=grp):
                        for fc in range(NFC):
                            ins = t.matmul(PSB[:, 0:512], wgu_slab[2 + gb][:, fc, :],
                                           y1_fm[:, fc, grp * 512:(grp + 1) * 512],
                                           start=(fc == 0), stop=(fc == NFC - 1))
                        return ins
                    t_umm = A("T", f_u, deps=[t_wu, y1fm_grp[grp](), rs["psB"]])
                    last_u = t_umm
                    t_si = A("S", lambda s: s.activation(tg_s, PSA[:, 0:512], AF.Silu),
                             deps=[t_gmm, rs["th1_prev"],
                                   last_scores_T if (ei == 0 and it == 0) else None])
                    rs["psA"] = t_si
                    t_h = A("V", (lambda it=it, grp=grp: lambda v: v.tensor_tensor(
                        h_sb[grp][:, it, :], tg_s, PSB[:, 0:512], ALU.mult))(),
                        deps=[t_si, t_umm, rs["h_last"][grp],
                              attn_done_T if (ei == 0 and it == 0) else None])
                    rs["psB"] = t_h
                    rs["th1_prev"] = t_h
                    rs["h_tok"][grp] = t_h
                rs["gu_buf_last"][gb] = last_u

        def emit_wd(ei, grp):
            wd_mm_last = None

            for tt in range(4):
                qt = grp * 4 + tt
                for fb in range(4):
                    def f_wd(t, grp=grp, tt=tt, fb=fb):
                        for ic in range(8):
                            ins = t.matmul(PSM[:, fb * 512:(fb + 1) * 512],
                                           h_sb[grp][:, ic, tt * 128:(tt + 1) * 128],
                                           wd_sb[:, ic, fb * 512:(fb + 1) * 512],
                                           start=(ic == 0), stop=(ic == 7))
                        return ins
                    t_wdm = A("T", f_wd, deps=[rs["h_tok"][grp], rs["wd_tok"],
                                               rs["psm"][fb]])
                    wd_mm_last = t_wdm
                    if ei == 0:
                        t_ac = A("V", (lambda qt=qt, fb=fb: lambda v: v.tensor_tensor(
                            y1_tm[qt][:, fb * 512:(fb + 1) * 512],
                            PSM[:, fb * 512:(fb + 1) * 512],
                            y1_tm[qt][:, fb * 512:(fb + 1) * 512], ALU.add))(),
                            deps=[t_wdm, y1acc[qt]])
                    else:
                        ci = qt * 8 + (ei - 1)
                        t_ac = A("V", (lambda qt=qt, fb=fb, ci=ci: lambda v:
                                       v.scalar_tensor_tensor(
                            y1_tm[qt][:, fb * 512:(fb + 1) * 512],
                            PSM[:, fb * 512:(fb + 1) * 512],
                            comb_flat[:, ci:ci + 1],
                            y1_tm[qt][:, fb * 512:(fb + 1) * 512],
                            ALU.mult, ALU.add))(),
                            deps=[t_wdm, y1acc[qt], combT_evac[qt]])
                    y1acc[qt] = t_ac
                    rs["psm"][fb] = t_ac
            rs["h_last"][grp] = wd_mm_last
            rs["wd_reader"] = wd_mm_last

        for qt in range(4):
            emit_ln1(qt)
            emit_router(qt)
        rs["psm"] = list(psm_slice_free)
        rs["gu_buf_last"] = [xown_last_use, xown_last_use]
        first_gu = [True]
        for qt in range(4, 8):
            its = {4: (0, 1), 5: (2, 3), 6: (4, 5), 7: (6, 7)}[qt]
            emit_gu_block(0, [0], load_wd=first_gu[0], its=its)
            first_gu[0] = False
            emit_ln1(qt)
            emit_router(qt)
        emit_wd(0, 0)
        emit_gu_block(0, [1], load_wd=False)
        emit_wd(0, 1)
        for ei in range(1, 9):
            emit_gu_block(ei, [0, 1], load_wd=True)
            emit_wd(ei, 0)
            emit_wd(ei, 1)

        gu_buf_last = rs["gu_buf_last"]
        wd_last_reader = rs["wd_reader"]

        ybf_free = list(gu_buf_last)
        for qt in range(8):
            extra = (ybf_free + [wd_last_reader]) if qt == 0 else []
            t_sh = ln_stats(y1_tm[qt], [y1acc[qt]], [y1acc[qt]] + extra)
            t_y2 = A("V", (lambda qt=qt: lambda v: v.tensor_scalar(
                y1_tm[qt], y1_tm[qt], ln_rstd, ln_negmean, ALU.mult, ALU.add))(),
                deps=[t_sh])
            A("Y", (lambda qt=qt: lambda y: y.dma_start(
                out=outp[qt * 128:(qt + 1) * 128, :], in_=y1_tm[qt]))(),
                deps=[t_y2], dma=True)

        with nc.Block() as block:
            def make_emitter(ename):
                def emit(eng):
                    wmax = {}
                    for fn, deps, inc in sch.ops[ename]:
                        for (de, dc) in deps:
                            if dc > wmax.get(de, 0):
                                eng.wait_ge(sems[de], dc)
                                wmax[de] = dc
                        ins = fn(eng)
                        ins.then_inc(sems[inc[0]], inc[1])
                    if ename == "Y":
                        for i in range(NLANES):
                            lane = f"Y{i}"
                            fin = sch.lane_cnt.get(lane, 0)
                            if fin:
                                eng.wait_ge(sems[lane], fin)
                return emit

            @block.tensor
            def _(t):
                make_emitter("T")(t)

            @block.vector
            def _(v):
                make_emitter("V")(v)

            @block.scalar
            def _(s):
                make_emitter("S")(s)

            @block.gpsimd
            def _(g):
                make_emitter("G")(g)

            @block.sync
            def _(y):
                make_emitter("Y")(y)

    return nc


_CACHED = {}


def _host_pack(inputs):
    q = np.asarray(inputs["L0_attn_query"], np.float32)
    Wqkv = np.asarray(inputs["L0_attn_Wqkv"], np.float32)
    Wproj = np.asarray(inputs["L0_attn_Wproj"], np.float32)
    bproj = np.asarray(inputs["L0_attn_bproj"], np.float32)
    Wr = np.asarray(inputs["L0_ffn_Wr"], np.float32)
    Wg = np.asarray(inputs["L0_ffn_Wg"], np.float32)
    Wu = np.asarray(inputs["L0_ffn_Wu"], np.float32)
    Wd = np.asarray(inputs["L0_ffn_Wd"], np.float32)
    Wg_s = np.asarray(inputs["L0_ffn_Wg_s"], np.float32)
    Wu_s = np.asarray(inputs["L0_ffn_Wu_s"], np.float32)
    Wd_s = np.asarray(inputs["L0_ffn_Wd_s"], np.float32)

    xT = np.ascontiguousarray(q.transpose(0, 2, 1))
    wp = np.ascontiguousarray(Wproj.T.reshape(NFC, 128, H))
    wr = np.ascontiguousarray(Wr.T.reshape(NFC, 128, 8).transpose(1, 0, 2).reshape(128, NFC * 8))
    Wg9 = np.concatenate([Wg_s[None], Wg], 0)
    Wu9 = np.concatenate([Wu_s[None], Wu], 0)
    Wd9 = np.concatenate([Wd_s[None], Wd], 0)
    wgp = np.ascontiguousarray(Wg9.reshape(9, NFC, 128, 8, 128).transpose(0, 3, 2, 1, 4))
    wup = np.ascontiguousarray(Wu9.reshape(9, NFC, 128, 8, 128).transpose(0, 3, 2, 1, 4))
    wdp = np.ascontiguousarray(Wd9.reshape(9, 8, 128, H))

    per_core = []
    for c in range(8):
        j, half = c // 2, c % 2
        s_lo = SH * half
        heads = [j, 4 + j, 8 + j, 12 + j]

        def pack_w(rows_off):
            mats = []
            for hd in heads:
                Wh = Wqkv[384 * hd + rows_off:384 * hd + rows_off + 128, :]
                mats.append(np.ascontiguousarray(Wh.T.reshape(NFC, 128, 128)))
            a = np.stack(mats, 0)
            return np.ascontiguousarray(a.transpose(2, 0, 1, 3).reshape(128, 4 * NFC * 128))
        per_core.append({
            "xT": xT,
            "xq": np.ascontiguousarray(xT[:, :, s_lo:s_lo + SH]),
            "xo": np.ascontiguousarray(q[j, s_lo:s_lo + SH, :]),
            "wq": pack_w(0),
            "wk": pack_w(128),
            "wv": pack_w(256),
            "wp": wp,
            "wr": wr,
            "ceye": np.eye(128, dtype=np.float32),
            "cone": np.ones((1, 128), np.float32),
            "cng2": np.full((128, 1), -2.0, np.float32),
            "bpj": bproj.reshape(1, H),
            "wgp": wgp,
            "wup": wup,
            "wdp": wdp,
        })
    return per_core


def kernel(**inputs):
    per_core = _host_pack(inputs)
    if "nc" not in _CACHED:
        _CACHED["nc"] = build_program()
    nc = _CACHED["nc"]
    res = run_bass_kernel_spmd(nc, per_core, core_ids=list(range(8)))
    out = np.empty((B, S, H), np.float32)
    for c in range(8):
        j, half = c // 2, c % 2
        out[j, SH * half:SH * (half + 1), :] = res.results[c]["out"]
    if DEBUG_TAPS:
        kernel.taps = [res.results[c] for c in range(8)]
    return out


# version 15
# speedup vs baseline: 1.0873x; 1.0873x over previous
import math
import numpy as np
from contextlib import ExitStack

import concourse.bass as bass
import concourse.mybir as mybir
from concourse.bass_utils import run_bass_kernel_spmd

F32 = mybir.dt.float32
BF16 = mybir.dt.bfloat16
AF = mybir.ActivationFunctionType
ALU = mybir.AluOpType
AX = mybir.AxisListType.X

B, S, H = 4, 2048, 2048
NH, HN = 16, 128
SH = 1024
NORM = 1.0 / math.sqrt(HN)
EPS = 1e-5
NFC = H // 128

DEBUG_TAPS = False
BISECT_NO_INTERLEAVE = False
BISECT_NO_LAG = False


NLANES = 8


class Sched:
    ENGS = ("T", "V", "S", "G", "Y")

    def __init__(self):
        self.ops = {e: [] for e in self.ENGS}
        self.cnt = {e: 0 for e in self.ENGS}
        self.lane_cnt = {}
        self.dma_idx = {"G": 0, "Y": 0}

    def add(self, e, fn, deps=(), dma=False):
        ds = [d for d in deps if d is not None]
        if dma and e in ("G", "Y"):
            lane = f"{e}{self.dma_idx[e] % NLANES}"
            self.dma_idx[e] += 1
            prev = self.lane_cnt.get(lane, 0)
            if prev:
                ds.append((lane, prev))
            self.lane_cnt[lane] = prev + 16
            self.ops[e].append((fn, ds, (lane, 16), True))
            return (lane, prev + 16)
        amt = 16 if dma else 1
        self.cnt[e] += amt
        tok = (e, self.cnt[e])
        self.ops[e].append((fn, ds, (e, amt), False))
        return tok


def build_program():
    nc = bass.Bass("TRN2", target_bir_lowering=False)

    xT = nc.dram_tensor("xT", [B, H, S], F32, kind="ExternalInput").ap()
    xq = nc.dram_tensor("xq", [B, H, SH], F32, kind="ExternalInput").ap()
    xo = nc.dram_tensor("xo", [SH, H], F32, kind="ExternalInput").ap()
    wq_d = nc.dram_tensor("wq", [128, 4 * NFC * 128], F32, kind="ExternalInput").ap()
    wk_d = nc.dram_tensor("wk", [128, 4 * NFC * 128], F32, kind="ExternalInput").ap()
    wv_d = nc.dram_tensor("wv", [128, 4 * NFC * 128], F32, kind="ExternalInput").ap()
    wp_d = nc.dram_tensor("wp", [NFC, 128, H], F32, kind="ExternalInput").ap()
    wr_d = nc.dram_tensor("wr", [128, NFC * 8], F32, kind="ExternalInput").ap()
    ceye = nc.dram_tensor("ceye", [128, 128], F32, kind="ExternalInput").ap()
    cone = nc.dram_tensor("cone", [1, 128], F32, kind="ExternalInput").ap()
    cng2 = nc.dram_tensor("cng2", [128, 1], F32, kind="ExternalInput").ap()
    bp_d = nc.dram_tensor("bpj", [1, H], F32, kind="ExternalInput").ap()
    wgp = nc.dram_tensor("wgp", [9, 8, 128, NFC, 128], F32, kind="ExternalInput").ap()
    wup = nc.dram_tensor("wup", [9, 8, 128, NFC, 128], F32, kind="ExternalInput").ap()
    wdp = nc.dram_tensor("wdp", [9, 8, 128, H], F32, kind="ExternalInput").ap()
    outp = nc.dram_tensor("out", [SH, H], F32, kind="ExternalOutput").ap()
    taps = {}
    if DEBUG_TAPS:
        taps["t_ctx"] = nc.dram_tensor("t_ctx", [128, 16 * 1024], BF16, kind="ExternalOutput").ap()
        taps["t_y1"] = nc.dram_tensor("t_y1", [SH, H], F32, kind="ExternalOutput").ap()
        taps["t_comb"] = nc.dram_tensor("t_comb", [8, 128, 8], F32, kind="ExternalOutput").ap()
        taps["t_kfm"] = nc.dram_tensor("t_kfm", [128, 2048], BF16, kind="ExternalOutput").ap()
        taps["t_vtm"] = nc.dram_tensor("t_vtm", [128, 2048], BF16, kind="ExternalOutput").ap()
        taps["t_qfm"] = nc.dram_tensor("t_qfm", [128, 1024], BF16, kind="ExternalOutput").ap()
        taps["t_pnt"] = nc.dram_tensor("t_pnt", [128, 4096], BF16, kind="ExternalOutput").ap()

    sch = Sched()
    A = sch.add

    with ExitStack() as ctx:
        e_ = ctx.enter_context

        POOL_BYTES = 212736
        pool = e_(nc.sbuf_tensor("pool", [128, POOL_BYTES // 2], BF16))

        def carve(off_b, shape, dtype):
            esz = 2 if dtype == BF16 else 4
            n = int(np.prod(shape[1:])) * esz
            assert off_b % 2 == 0 and off_b + n <= POOL_BYTES, (off_b, n)
            v = pool.ap()[:, off_b // 2:(off_b + n) // 2]
            if dtype != BF16:
                v = v.bitcast(dtype)
            if len(shape) == 3:
                v = v.rearrange("p (a b) -> p a b", a=shape[1], b=shape[2])
            elif len(shape) == 4:
                v = v.rearrange("p (a b c) -> p a b c", a=shape[1], b=shape[2], c=shape[3])
            if shape[0] != 128:
                v = v[0:shape[0]]
            return v

        o_consts = 0
        o_A0 = 6912
        o_A1 = 23296
        o_C = 39680
        o_K = 88832
        o_V = 121600
        o_Q = 154368
        o_VS = 162560
        o_PT = 163584
        o_PP = 171776
        o_CX = 179968
        o_BPR = 23296 + 8192
        assert o_CX + 32768 <= POOL_BYTES

        ident = carve(o_consts, [128, 128], BF16)
        wrp_sb = carve(o_consts + 2304, [128, NFC, 8], BF16)
        ones_bf = carve(o_consts + 2560, [1, 128], BF16)
        combT = carve(o_consts + 2816, [8, 1024], BF16)
        so = [o_consts + 4864]

        def small(n=1, dtype=F32):
            esz = 4 if dtype == F32 else 2
            o = so[0]
            so[0] += max(((n * esz + 7) // 8) * 8, 8)
            assert so[0] <= 6912
            return carve(o, [128, n], dtype)

        negmax = [small(), small()]
        bias_t = [small(), small()]
        neg2 = small()
        ell = [small(), small()]
        ellA = [small(), small()]
        ellB = [small(), small()]
        rell = [small(), small()]
        ln_negsum = small(); ln_negmean = small(); ln_ssq = small()
        ln_vs = small(); ln_sd = small(); ln_rstd = small()
        r_m1 = small(); r_m2 = small(); r_dd = small(); r_w1 = small(); r_w2 = small()
        r_d1 = small(8); r_i1 = small(8); r_l2 = small(8); r_d2 = small(8)
        r_i2 = small(8); r_t1 = small(8); r_comb = small(8); r_combbf = small(8, BF16)
        comb_flat = small(64)

        xslab = [carve(o_A0, [128, NFC, 512], BF16), carve(o_A1, [128, NFC, 512], BF16)]
        wq_sb = carve(o_C, [128, 4, NFC, 128], BF16)
        wk_sb = carve(o_C + 16384, [128, 4, NFC, 128], BF16)
        wv_sb = carve(o_C + 32768, [128, 4, NFC, 128], BF16)
        k_fm = [[carve(o_K + pb * 16384 + h * 4096, [128, 2048], BF16) for h in range(4)]
                for pb in range(2)]
        v_tm = [[carve(o_V + pb * 16384 + h * 4096, [128, 16, 128], BF16) for h in range(4)]
                for pb in range(2)]
        q_fm = [carve(o_Q + h * 2048, [128, 1024], BF16) for h in range(4)]
        vscr = carve(o_VS, [128, 512], BF16)
        bpr = carve(o_BPR, [1, 2048], BF16)
        PnT = carve(o_PT, [128, 16, 256], BF16)
        Pp = [carve(o_PP, [128, 2048], BF16), carve(o_PP + 4096, [128, 2048], BF16)]
        ctx_sb = carve(o_CX, [128, 16, 1024], BF16)

        x_own = carve(o_A0, [128, 2048], F32)
        wgu_slab = [carve(o_A0 + i * 4096, [128, NFC, 128], BF16) for i in range(4)]
        wp_sb = [carve(o_K, [128, NFC, 512], BF16), carve(o_K + 16384, [128, NFC, 512], BF16)]
        y1_fm = carve(o_K, [128, NFC, 1024], BF16)
        yc = carve(o_V, [128, 2048], F32)
        y1bf = carve(o_V + 8192, [128, 2048], BF16)
        h_sb = [carve(o_V + 12288, [128, 8, 512], BF16), carve(o_V + 20480, [128, 8, 512], BF16)]
        cb_sb = [carve(o_V + 28672, [128, 512], F32), carve(o_V + 30720, [128, 512], F32)]
        tg_s = carve(o_Q, [128, 512], F32)
        th_s = carve(o_Q + 2048, [128, 512], F32)
        wd_sb = carve(o_CX, [128, 8, 2048], BF16)
        y1_tm = [carve(o_C + i * 8192, [128, 2048], F32) for i in range(6)] + \
                [carve(o_PT, [128, 2048], F32), carve(o_A1, [128, 2048], F32)]

        PSM = e_(nc.psum_tensor("PSM", [128, 2048], F32))
        PSA = e_(nc.psum_tensor("PSA", [128, 512], F32))
        PSB = e_(nc.psum_tensor("PSB", [128, 512], F32))
        PTA = e_(nc.psum_tensor("PTA", [128, 1024], BF16))
        PSC = e_(nc.psum_tensor("PSC", [128, 512], F32))

        sems = {e: e_(nc.semaphore("sem" + e)) for e in Sched.ENGS}
        for e in ("G", "Y"):
            for i in range(NLANES):
                sems[f"{e}{i}"] = e_(nc.semaphore(f"sem{e}{i}"))

        t_ident = A("G", lambda g: g.dma_start(out=ident, in_=ceye), dma=True)
        t_ones = A("G", lambda g: g.dma_start(out=ones_bf, in_=cone), dma=True)
        t_neg2 = A("G", lambda g: g.dma_start(out=neg2, in_=cng2), dma=True)

        t_wq = A("G", lambda g: g.dma_start(out=wq_sb, in_=wq_d.rearrange(
            "p (a b c) -> p a b c", a=4, b=NFC, c=128)), dma=True)
        t_wk = A("G", lambda g: g.dma_start(out=wk_sb, in_=wk_d.rearrange(
            "p (a b c) -> p a b c", a=4, b=NFC, c=128)), dma=True)
        t_wv = A("G", lambda g: g.dma_start(out=wv_sb, in_=wv_d.rearrange(
            "p (a b c) -> p a b c", a=4, b=NFC, c=128)), dma=True)
        t_wr = A("G", lambda g: g.dma_start(out=wrp_sb, in_=wr_d.rearrange(
            "p (a b) -> p a b", a=NFC, b=8)), dma=True)
        t_bpr = None

        st = {
            "slab_last": [None, None],
            "psA": None, "psB": None, "ptA": None, "psC": None,
            "sl": 0,
            "slab_tok": None, "mm_tok": None, "vev_tok": None, "vscr_war": None,
            "qkv_last_T": None,
        }
        kvq_done = [None] * B
        grp_scores_last = {}
        grp_av_last = {}

        MMC = 4

        def make_kv_steps(bp):
            pb = bp % 2
            steps = []
            kwar = lambda r: grp_scores_last.get((bp - 2, r))
            vwar = lambda r: grp_av_last.get((bp - 2, r))

            def slab_load(src, sidx):
                def go():
                    st["slab_tok"] = A("G", (lambda: lambda g: g.dma_start(
                        out=xslab[sidx], in_=src))(), deps=[st["slab_last"][sidx]], dma=True)
                return go

            def mm_chunk(kind, hd, sidx, c0, wsb):
                def go(kind=kind, hd=hd, sidx=sidx, c0=c0, wsb=wsb):
                    ps = PSB if kind == "v" else PSA

                    def f(t):
                        for fc in range(c0, c0 + MMC):
                            ins = t.matmul(ps[:, 0:512], wsb[:, hd, fc, :],
                                           xslab[sidx][:, fc, :],
                                           start=(fc == 0), stop=(fc == NFC - 1))
                        return ins
                    deps = [st["slab_tok"], st["psB"] if kind == "v" else st["psA"]]
                    if c0 == 0:
                        deps.append(t_wv if kind == "v" else t_wk)
                    tok = A("T", f, deps=deps)
                    st["qkv_last_T"] = tok
                    st["mm_tok"] = tok
                    if c0 + MMC == NFC:
                        if kind == "v":
                            ev = A("V", (lambda: lambda v: v.tensor_copy(
                                vscr, PSB[:, 0:512]))(), deps=[tok, st["vscr_war"]])
                            st["psB"] = ev
                            st["vev_tok"] = ev
                        else:
                            tg = go.tg
                            ev = A("V", (lambda hd=hd, tg=tg: lambda v: v.tensor_copy(
                                k_fm[pb][hd][:, tg * 512:(tg + 1) * 512], PSA[:, 0:512]))(),
                                deps=[tok, kwar(hd) if tg == 0 else None])
                            st["psA"] = ev
                            kvq_done[bp] = ev
                return go

            def v_transpose(hd, tg):
                def go(hd=hd, tg=tg):
                    def f(t):
                        for tb in range(4):
                            ins = t.transpose(PTA[:, tb * 128:(tb + 1) * 128],
                                              vscr[:, tb * 128:(tb + 1) * 128], ident)
                        return ins
                    tok = A("T", f, deps=[st["vev_tok"], t_ident, st["ptA"]])
                    st["qkv_last_T"] = tok
                    st["vscr_war"] = tok
                    ev = A("V", (lambda hd=hd, tg=tg: lambda v: v.tensor_copy(
                        v_tm[pb][hd][:, tg * 4:tg * 4 + 4, :],
                        PTA[:, 0:512].rearrange("p (a b) -> p a b", a=4)))(),
                        deps=[tok, vwar(hd) if tg == 0 else None])
                    st["ptA"] = ev
                    kvq_done[bp] = ev
                return go

            for tg in range(4):
                sidx = st["sl"] % 2
                src = xT[bp].rearrange("(fc p) s -> p fc s", p=128)[:, :, tg * 512:(tg + 1) * 512]
                steps.append(slab_load(src, sidx))
                for hd in range(4):
                    for c0 in range(0, NFC, MMC):
                        g_ = mm_chunk("k", hd, sidx, c0, wk_sb)
                        g_.tg = tg
                        steps.append(g_)
                    for c0 in range(0, NFC, MMC):
                        g_ = mm_chunk("v", hd, sidx, c0, wv_sb)
                        g_.tg = tg
                        steps.append(g_)
                    steps.append(v_transpose(hd, tg))
                steps.append((lambda sidx=sidx: (lambda: st["slab_last"].__setitem__(
                    sidx, st["qkv_last_T"])))())
                st["sl"] += 1
            return steps

        def emit_q_block(bp):
            for qg in range(2):
                sidx = st["sl"] % 2
                src = xq[bp].rearrange("(fc p) s -> p fc s", p=128)[:, :, qg * 512:(qg + 1) * 512]
                st["slab_tok"] = A("G", (lambda src=src, sidx=sidx: lambda g: g.dma_start(
                    out=xslab[sidx], in_=src))(), deps=[st["slab_last"][sidx]], dma=True)
                for hd in range(4):
                    def f_q(t, hd=hd, sidx=sidx):
                        for fc in range(NFC):
                            ins = t.matmul(PSA[:, 0:512], wq_sb[:, hd, fc, :],
                                           xslab[sidx][:, fc, :],
                                           start=(fc == 0), stop=(fc == NFC - 1))
                        return ins
                    t_qmm = A("T", f_q, deps=[st["slab_tok"], t_wq, st["psA"],
                                              grp_scores_last.get((bp - 1, hd)) if qg == 0 else None])
                    st["qkv_last_T"] = t_qmm
                    t_qev = A("V", (lambda hd=hd, qg=qg: lambda v: v.tensor_copy(
                        q_fm[hd][:, qg * 512:(qg + 1) * 512], PSA[:, 0:512]))(), deps=[t_qmm])
                    st["psA"] = t_qev
                    kvq_done[bp] = t_qev
                st["slab_last"][sidx] = t_qmm
                st["sl"] += 1

        def drain(steps, n):
            for _ in range(min(n, len(steps))):
                steps.pop(0)()
            return steps

        tok_exp = {}
        tok_expA = {}
        tok_norm = {}
        tok_trans = {}
        tok_evacB = {}
        tok_av = {}
        ctx_evac = {}
        tok_exp_prev = None
        gqt = 0

        def emit_trans_av(g):
            bp_g = g // 32
            r_g = (g // 8) % 4
            qt_g = g % 8
            pb_g = bp_g % 2
            islot = 4 * bp_g + r_g
            par = g % 2
            pr = qt_g % 2

            def f_trA(t, par=par):
                for c in range(8):
                    ins = t.transpose(PTA[:, c * 128:(c + 1) * 128],
                                      Pp[par][:, c * 128:(c + 1) * 128], ident)
                return ins
            t_trA = A("T", f_trA, deps=[tok_norm[g], st["ptA"], t_ident])
            t_evA = A("V", (lambda pr=pr: lambda v: v.tensor_copy(
                PnT[:, 0:8, pr * 128:(pr + 1) * 128],
                PTA[:, 0:1024].rearrange("p (a b) -> p a b", a=8)))(),
                deps=[t_trA, tok_av.get(g // 2 - 1)])
            st["ptA"] = t_evA

            def f_trB(t, par=par):
                for c in range(8):
                    ins = t.transpose(PTA[:, c * 128:(c + 1) * 128],
                                      Pp[par][:, (8 + c) * 128:(9 + c) * 128], ident)
                return ins
            t_trB = A("T", f_trB, deps=[tok_norm[g], t_evA, t_ident])
            tok_trans[g] = t_trB
            t_evB = A("V", (lambda pr=pr: lambda v: v.tensor_copy(
                PnT[:, 8:16, pr * 128:(pr + 1) * 128],
                PTA[:, 0:1024].rearrange("p (a b) -> p a b", a=8)))(),
                deps=[t_trB, tok_av.get(g // 2 - 1)])
            st["ptA"] = t_evB
            tok_evacB[g] = t_evB

            if qt_g % 2 == 1:
                pair = g // 2

                def f_av(t, hd=r_g, pb=pb_g):
                    for kc in range(16):
                        ins = t.matmul(PSC[:, 0:256], v_tm[pb][hd][:, kc, :],
                                       PnT[:, kc, :], start=(kc == 0), stop=(kc == 15))
                    return ins
                t_avmm = A("T", f_av, deps=[tok_evacB[g - 1], t_evB, st["psC"]])
                tok_av[pair] = t_avmm
                grp_av_last[(bp_g, r_g)] = t_avmm
                pq = qt_g // 2
                t_cxe = A("V", (lambda islot=islot, pq=pq: lambda v: v.tensor_copy(
                    ctx_sb[:, islot, pq * 256:(pq + 1) * 256], PSC[:, 0:256]))(),
                    deps=[t_avmm])
                ctx_evac[pair] = t_cxe
                st["psC"] = t_cxe

        pend = make_kv_steps(0)
        while pend:
            pend = drain(pend, 1000)
        emit_q_block(0)
        if DEBUG_TAPS:
            A("Y", lambda y: y.dma_start(out=taps["t_kfm"], in_=k_fm[0][0]),
              deps=[kvq_done[0]], dma=True)
            A("Y", lambda y: y.dma_start(
                out=taps["t_vtm"], in_=v_tm[0][0].rearrange("p a b -> p (a b)")),
              deps=[kvq_done[0]], dma=True)
            A("Y", lambda y: y.dma_start(out=taps["t_qfm"], in_=q_fm[0]),
              deps=[kvq_done[0]], dma=True)

        for bp in range(B):
            pb = bp % 2
            nxt = make_kv_steps(bp + 1) if bp + 1 < B else []
            if BISECT_NO_INTERLEAVE:
                while nxt:
                    nxt = drain(nxt, 1000)
            drain_calls = [32]
            my_kvq = kvq_done[bp]

            for r in range(4):
                hd = r
                islot = 4 * bp + r
                for qt in range(8):
                    par = gqt % 2

                    def f_scA(t, hd=hd, qt=qt, pb=pb):
                        for fb in range(2):
                            ins = t.matmul(PSM[:, fb * 512:(fb + 1) * 512],
                                           q_fm[hd][:, qt * 128:(qt + 1) * 128],
                                           k_fm[pb][hd][:, fb * 512:(fb + 1) * 512],
                                           start=True, stop=True)
                        return ins
                    t_scA = A("T", f_scA, deps=[tok_expA.get(gqt - 1), my_kvq])

                    def f_scB(t, hd=hd, qt=qt, pb=pb):
                        for fb in range(2, 4):
                            ins = t.matmul(PSM[:, fb * 512:(fb + 1) * 512],
                                           q_fm[hd][:, qt * 128:(qt + 1) * 128],
                                           k_fm[pb][hd][:, fb * 512:(fb + 1) * 512],
                                           start=True, stop=True)
                        return ins
                    t_scB = A("T", f_scB, deps=[tok_exp_prev, my_kvq])
                    grp_scores_last[(bp, r)] = t_scB
                    t_mx = A("V", (lambda par=par: lambda v: v.reduce_max(
                        negmax[par], PSM[:, 0:256], axis=AX, negate=True))(),
                        deps=[t_scA, tok_exp.get(gqt - 2)])
                    t_bi = A("V", (lambda par=par: lambda v: v.tensor_scalar(
                        bias_t[par], negmax[par], NORM, -2.0, ALU.mult, ALU.add))(),
                        deps=[t_mx])
                    t_exA = A("S", (lambda par=par: lambda s: s.activation(
                        Pp[par][:, 0:1024], PSM[:, 0:1024], AF.Exp, bias=bias_t[par],
                        scale=NORM, accum_out=ellA[par]))(),
                        deps=[t_bi, tok_trans.get(gqt - 2), tok_norm.get(gqt - 2)])
                    tok_expA[gqt] = t_exA
                    t_exB = A("S", (lambda par=par: lambda s: s.activation(
                        Pp[par][:, 1024:2048], PSM[:, 1024:2048], AF.Exp,
                        bias=bias_t[par], scale=NORM, accum_out=ellB[par]))(),
                        deps=[t_scB])
                    tok_exp[gqt] = t_exB
                    tok_exp_prev = t_exB

                    nxt = drain(nxt, -(-len(nxt) // max(1, drain_calls[0])))
                    drain_calls[0] -= 1

                    if gqt >= 1:
                        emit_trans_av(gqt - 1)

                    t_el = A("V", (lambda par=par: lambda v: v.tensor_add(
                        ell[par], ellA[par], ellB[par]))(), deps=[t_exB])
                    t_rc = A("V", (lambda par=par: lambda v: v.reciprocal(
                        rell[par], ell[par]))(), deps=[t_el])
                    t_no = A("V", (lambda par=par: lambda v: v.tensor_scalar(
                        Pp[par], Pp[par], rell[par], None, ALU.mult))(), deps=[t_rc])
                    tok_norm[gqt] = t_no
                    gqt += 1

            if bp == B - 1 and not BISECT_NO_LAG:
                emit_trans_av(gqt - 1)
            while nxt:
                nxt = drain(nxt, 1000)
            if bp + 1 < B:
                emit_q_block(bp + 1)


        attn_done_V = ctx_evac[gqt // 2 - 1]
        attn_done_S = tok_exp[gqt - 1]
        attn_done_T = tok_av[gqt // 2 - 1]
        qkv_last_T = st["qkv_last_T"]
        last_scores_T = grp_scores_last[(B - 1, 3)]

        if DEBUG_TAPS:
            A("Y", lambda y: y.dma_start(out=taps["t_ctx"],
                                         in_=carve(o_CX, [128, 16 * 1024], BF16)),
              deps=[attn_done_V], dma=True)

        t_bpr = A("G", lambda g: g.dma_start(out=bpr, in_=bp_d),
                  deps=[qkv_last_T], dma=True)
        wp_buf_last = [grp_scores_last[(B - 2, 3)], grp_scores_last[(B - 1, 3)]]
        rs = {"psA": st["psA"], "psB": st["psB"], "ptA": st["ptA"], "psC": st["psC"],
              "psm": None,
              "y1bf_last": None, "sq_prev": None, "yc_war": None, "routerV": None,
              "combbf_last": None, "combT_done": None,
              "gu_buf_last": [None, None],
              "th1_prev": None, "h_last": [None, None], "h_tok": [None, None],
              "wd_reader": None, "wd_tok": None, "cb": [None, None]}
        y1_tok = [None] * 8
        y1fm_tok = [None] * 8
        y1acc = {}
        combT_evac = [None] * 8
        xb_tok = [None] * 8
        proj_evt = {}
        psm_slice_free = [attn_done_S] * 4
        xown_last_use = None
        proj_last_T = None

        def ln_stats(src_tile, deps_v, deps_s):
            t_ns = A("V", lambda v: v.reduce_sum(
                ln_negsum, src_tile, axis=AX, negate=True), deps=deps_v)
            t_sq = A("S", lambda s: s.activation(
                y1bf, src_tile, AF.Square, accum_out=ln_ssq),
                deps=deps_s + [rs["y1bf_last"], rs.get("ssq_reader")])
            rs["sq_prev"] = t_sq
            t_nm = A("V", lambda v: v.tensor_scalar(
                ln_negmean, ln_negsum, 1.0 / H, None, ALU.mult), deps=[t_ns])
            t_vs = A("V", lambda v: v.tensor_scalar(
                ln_vs, ln_ssq, 1.0 / H, EPS, ALU.mult, ALU.add), deps=[t_sq, t_nm])
            rs["ssq_reader"] = t_vs
            t_m2 = A("V", lambda v: v.tensor_tensor(
                ln_sd, ln_negmean, ln_negmean, ALU.mult), deps=[t_vs])
            t_va = A("V", lambda v: v.tensor_sub(ln_vs, ln_vs, ln_sd), deps=[t_m2])
            t_sd = A("S", lambda s: s.activation(ln_sd, ln_vs, AF.Sqrt), deps=[t_va])
            t_rs_ = A("V", lambda v: v.reciprocal(ln_rstd, ln_sd), deps=[t_sd])
            t_sh = A("V", lambda v: v.tensor_tensor(
                ln_negmean, ln_negmean, ln_rstd, ALU.mult), deps=[t_rs_])
            return t_sh

        def emit_ln1_chain(qt):
            t_sh = ln_stats(y1_tm[qt], [proj_evt[(qt, 3)], rs["yc_war"]],
                            [proj_evt[(qt, 3)]])
            t_y1 = A("V", (lambda qt=qt: lambda v: v.tensor_scalar(
                y1_tm[qt], y1_tm[qt], ln_rstd, ln_negmean, ALU.mult, ALU.add))(),
                deps=[t_sh])
            y1_tok[qt] = t_y1
            y1acc[qt] = t_y1
            rs["yc_war"] = t_y1

        def emit_ln1(qt):
            t_yb = A("V", (lambda qt=qt: lambda v: v.tensor_copy(
                y1bf, y1_tm[qt]))(), deps=[y1_tok[qt], rs["y1bf_last"]])

            def f_t1(t):
                for c in range(8):
                    ins = t.transpose(PTA[:, c * 128:(c + 1) * 128],
                                      y1bf[:, c * 128:(c + 1) * 128], ident)
                return ins
            t_t1 = A("T", f_t1, deps=[t_yb, rs["ptA"]])
            t_e1 = A("S", (lambda qt=qt: lambda s: s.copy(
                y1_fm[:, 0:8, qt * 128:(qt + 1) * 128],
                PTA[:, 0:1024].rearrange("p (a b) -> p a b", a=8)))(),
                deps=[t_t1, proj_last_T if qt == 0 else None])
            rs["ptA"] = t_e1

            def f_t2(t):
                for c in range(8):
                    ins = t.transpose(PTA[:, c * 128:(c + 1) * 128],
                                      y1bf[:, (8 + c) * 128:(9 + c) * 128], ident)
                return ins
            t_t2 = A("T", f_t2, deps=[t_yb, rs["ptA"]])
            rs["y1bf_last"] = t_t2
            t_e2 = A("S", (lambda qt=qt: lambda s: s.copy(
                y1_fm[:, 8:16, qt * 128:(qt + 1) * 128],
                PTA[:, 0:1024].rearrange("p (a b) -> p a b", a=8)))(), deps=[t_t2])
            rs["ptA"] = t_e2
            y1fm_tok[qt] = t_e2
            if DEBUG_TAPS:
                A("Y", (lambda qt=qt: lambda y: y.dma_start(
                    out=taps["t_y1"][qt * 128:(qt + 1) * 128, :], in_=y1_tm[qt]))(),
                  deps=[t_y1], dma=True)

        def emit_router(qt):
            def f_lg(t, qt=qt):
                for fc in range(NFC):
                    ins = t.matmul(PSC[:, 0:8], y1_fm[:, fc, qt * 128:(qt + 1) * 128],
                                   wrp_sb[:, fc, :], start=(fc == 0), stop=(fc == NFC - 1))
                return ins
            t_lg = A("T", f_lg, deps=[y1fm_tok[qt], t_wr, rs["psC"]])
            t1 = A("V", lambda v: v.reduce_max(r_m1, PSC[:, 0:8], axis=AX),
                   deps=[t_lg, rs["routerV"]])
            t2 = A("V", lambda v: v.tensor_scalar(r_d1, PSC[:, 0:8], r_m1, None, ALU.subtract),
                   deps=[t1])
            t3 = A("V", lambda v: v.tensor_scalar(r_i1, r_d1, 0.0, None, ALU.is_ge), deps=[t2])
            t4 = A("V", lambda v: v.scalar_tensor_tensor(r_l2, r_i1, -1e9, PSC[:, 0:8],
                                                         ALU.mult, ALU.add), deps=[t3])
            t5 = A("V", lambda v: v.reduce_max(r_m2, r_l2, axis=AX), deps=[t4])
            t6 = A("V", lambda v: v.tensor_scalar(r_d2, r_l2, r_m2, None, ALU.subtract), deps=[t5])
            t7 = A("V", lambda v: v.tensor_scalar(r_i2, r_d2, 0.0, None, ALU.is_ge), deps=[t6])
            t8 = A("V", lambda v: v.tensor_sub(r_dd, r_m2, r_m1), deps=[t7])
            t9 = A("S", lambda s: s.activation(r_w1, r_dd, AF.Sigmoid, scale=-1.0), deps=[t8])
            t10 = A("V", lambda v: v.tensor_scalar(r_w2, r_w1, -1.0, 1.0, ALU.mult, ALU.add),
                    deps=[t9])
            t11 = A("V", lambda v: v.tensor_scalar(r_t1, r_i1, r_w1, None, ALU.mult), deps=[t10])
            t12 = A("V", lambda v: v.scalar_tensor_tensor(r_comb, r_i2, r_w2, r_t1,
                                                          ALU.mult, ALU.add), deps=[t11])
            rs["psC"] = t12
            t13 = A("V", (lambda qt=qt: lambda v: v.tensor_copy(
                comb_flat[:, qt * 8:(qt + 1) * 8], r_comb))(), deps=[t12])
            rs["routerV"] = t13
            combT_evac[qt] = t13
            if DEBUG_TAPS:
                A("Y", (lambda qt=qt: lambda y: y.dma_start(
                    out=taps["t_comb"][qt], in_=r_comb))(), deps=[t12], dma=True)

        for qp in range(4):
            bidx = qp % 2
            t_wp = A("G", (lambda qp=qp, bidx=bidx: lambda g: g.dma_start(
                out=wp_sb[bidx], in_=wp_d[:, :, qp * 512:(qp + 1) * 512].rearrange(
                    "a p s -> p a s")))(), deps=[wp_buf_last[bidx]], dma=True)
            t_pj_last = None
            for qt in range(8):
                if qp == 0:
                    t_xo = A("Y", (lambda qt=qt: lambda y: y.dma_start(
                        out=y1_tm[qt], in_=xo[qt * 128:(qt + 1) * 128, :]))(),
                        deps=[qkv_last_T if qt == 0 else None,
                              attn_done_T if qt == 0 else None,
                              attn_done_S if qt == 0 else None], dma=True)
                    xb_tok[qt] = t_xo
                    xown_last_use = t_xo
                sli = qt % 4

                def f_pj(t, qt=qt, bidx=bidx, sli=sli, qp=qp):
                    for i in range(16):
                        t.matmul(PSM[:, sli * 512:(sli + 1) * 512],
                                 ctx_sb[:, i, qt * 128:(qt + 1) * 128],
                                 wp_sb[bidx][:, i, :],
                                 start=(i == 0), stop=False)
                    return t.matmul(PSM[:, sli * 512:(sli + 1) * 512], ones_bf,
                                    bpr[:, qp * 512:(qp + 1) * 512],
                                    start=False, stop=True)
                t_pj = A("T", f_pj, deps=[t_wp, attn_done_V, psm_slice_free[sli],
                                          t_ones, t_bpr])
                t_pj_last = t_pj
                proj_last_T = t_pj
                t_ev = A("V", (lambda qt=qt, qp=qp, sli=sli: lambda v: v.tensor_tensor(
                    y1_tm[qt][:, qp * 512:(qp + 1) * 512], PSM[:, sli * 512:(sli + 1) * 512],
                    y1_tm[qt][:, qp * 512:(qp + 1) * 512], ALU.add))(),
                    deps=[t_pj, xb_tok[qt]])
                proj_evt[(qt, qp)] = t_ev
                psm_slice_free[sli] = t_ev
                if qp == 3:
                    emit_ln1_chain(qt)
            wp_buf_last[bidx] = t_pj_last


        y1fm_grp = {0: lambda: y1fm_tok[3], 1: lambda: y1fm_tok[7]}

        def emit_gu_block(ei, grps, load_wd, its=tuple(range(8))):
            if load_wd:
                rs["_wd_pend"] = ei
                rs["_wd_cnt"] = 0
            for it in its:
                gb = it % 2
                t_wg = A("G", (lambda ei=ei, it=it, gb=gb: lambda g: g.dma_start(
                    out=wgu_slab[gb], in_=wgp[ei, it]))(), deps=[rs["gu_buf_last"][gb]],
                    dma=True)
                t_wu = A("G", (lambda ei=ei, it=it, gb=gb: lambda g: g.dma_start(
                    out=wgu_slab[2 + gb], in_=wup[ei, it]))(), deps=[rs["gu_buf_last"][gb]],
                    dma=True)
                if rs.get("_wd_pend") is not None:
                    rs["_wd_cnt"] += 1
                    cnt = rs["_wd_cnt"]
                    wei = rs["_wd_pend"]
                    if cnt <= 8:
                        rs["wd_tok"] = A("G", (lambda wei=wei, ic=cnt - 1: lambda g:
                                              g.dma_start(out=wd_sb[:, ic, :],
                                                          in_=wdp[wei, ic]))(),
                            deps=[rs["wd_reader"],
                                  proj_last_T if wei == 0 else None],
                            dma=True)
                    if cnt == 8 or (it == its[-1] and ei != 0 and cnt < 8):
                        while rs["_wd_cnt"] < 8:
                            rs["_wd_cnt"] += 1
                            ic = rs["_wd_cnt"] - 1
                            rs["wd_tok"] = A("G", (lambda wei=wei, ic=ic: lambda g:
                                                  g.dma_start(out=wd_sb[:, ic, :],
                                                              in_=wdp[wei, ic]))(),
                                deps=[rs["wd_reader"],
                                      proj_last_T if wei == 0 else None], dma=True)
                        rs["_wd_pend"] = None
                last_u = None
                for grp in grps:
                    def f_g(t, gb=gb, grp=grp):
                        for fc in range(NFC):
                            ins = t.matmul(PSA[:, 0:512], wgu_slab[gb][:, fc, :],
                                           y1_fm[:, fc, grp * 512:(grp + 1) * 512],
                                           start=(fc == 0), stop=(fc == NFC - 1))
                        return ins
                    t_gmm = A("T", f_g, deps=[t_wg, y1fm_grp[grp](), rs["psA"]])

                    def f_u(t, gb=gb, grp=grp):
                        for fc in range(NFC):
                            ins = t.matmul(PSB[:, 0:512], wgu_slab[2 + gb][:, fc, :],
                                           y1_fm[:, fc, grp * 512:(grp + 1) * 512],
                                           start=(fc == 0), stop=(fc == NFC - 1))
                        return ins
                    t_umm = A("T", f_u, deps=[t_wu, y1fm_grp[grp](), rs["psB"]])
                    last_u = t_umm
                    t_si = A("S", lambda s: s.activation(tg_s, PSA[:, 0:512], AF.Silu),
                             deps=[t_gmm, rs["th1_prev"],
                                   last_scores_T if (ei == 0 and it == 0) else None])
                    rs["psA"] = t_si
                    t_h = A("V", (lambda it=it, grp=grp: lambda v: v.tensor_tensor(
                        h_sb[grp][:, it, :], tg_s, PSB[:, 0:512], ALU.mult))(),
                        deps=[t_si, t_umm, rs["h_last"][grp],
                              attn_done_T if (ei == 0 and it == 0) else None])
                    rs["psB"] = t_h
                    rs["th1_prev"] = t_h
                    rs["h_tok"][grp] = t_h
                rs["gu_buf_last"][gb] = last_u

        def emit_wd(ei, grp):
            wd_mm_last = None

            for tt in range(4):
                qt = grp * 4 + tt
                for fb in range(4):
                    def f_wd(t, grp=grp, tt=tt, fb=fb):
                        for ic in range(8):
                            ins = t.matmul(PSM[:, fb * 512:(fb + 1) * 512],
                                           h_sb[grp][:, ic, tt * 128:(tt + 1) * 128],
                                           wd_sb[:, ic, fb * 512:(fb + 1) * 512],
                                           start=(ic == 0), stop=(ic == 7))
                        return ins
                    t_wdm = A("T", f_wd, deps=[rs["h_tok"][grp], rs["wd_tok"],
                                               rs["psm"][fb]])
                    wd_mm_last = t_wdm
                    if ei == 0:
                        t_ac = A("V", (lambda qt=qt, fb=fb: lambda v: v.tensor_tensor(
                            y1_tm[qt][:, fb * 512:(fb + 1) * 512],
                            PSM[:, fb * 512:(fb + 1) * 512],
                            y1_tm[qt][:, fb * 512:(fb + 1) * 512], ALU.add))(),
                            deps=[t_wdm, y1acc[qt]])
                    else:
                        ci = qt * 8 + (ei - 1)
                        t_ac = A("V", (lambda qt=qt, fb=fb, ci=ci: lambda v:
                                       v.scalar_tensor_tensor(
                            y1_tm[qt][:, fb * 512:(fb + 1) * 512],
                            PSM[:, fb * 512:(fb + 1) * 512],
                            comb_flat[:, ci:ci + 1],
                            y1_tm[qt][:, fb * 512:(fb + 1) * 512],
                            ALU.mult, ALU.add))(),
                            deps=[t_wdm, y1acc[qt], combT_evac[qt]])
                    y1acc[qt] = t_ac
                    rs["psm"][fb] = t_ac
            rs["h_last"][grp] = wd_mm_last
            rs["wd_reader"] = wd_mm_last

        for qt in range(4):
            emit_ln1(qt)
            emit_router(qt)
        rs["psm"] = list(psm_slice_free)
        rs["gu_buf_last"] = [xown_last_use, xown_last_use]
        first_gu = [True]
        for qt in range(4, 8):
            its = {4: (0, 1), 5: (2, 3), 6: (4, 5), 7: (6, 7)}[qt]
            emit_gu_block(0, [0], load_wd=first_gu[0], its=its)
            first_gu[0] = False
            emit_ln1(qt)
            emit_router(qt)
        emit_wd(0, 0)
        emit_gu_block(0, [1], load_wd=False)
        emit_wd(0, 1)
        for ei in range(1, 9):
            emit_gu_block(ei, [0, 1], load_wd=True)
            emit_wd(ei, 0)
            emit_wd(ei, 1)

        gu_buf_last = rs["gu_buf_last"]
        wd_last_reader = rs["wd_reader"]

        ybf_free = list(gu_buf_last)
        for qt in range(8):
            extra = (ybf_free + [wd_last_reader]) if qt == 0 else []
            t_sh = ln_stats(y1_tm[qt], [y1acc[qt]], [y1acc[qt]] + extra)
            t_y2 = A("V", (lambda qt=qt: lambda v: v.tensor_scalar(
                y1_tm[qt], y1_tm[qt], ln_rstd, ln_negmean, ALU.mult, ALU.add))(),
                deps=[t_sh])
            A("Y", (lambda qt=qt: lambda y: y.dma_start(
                out=outp[qt * 128:(qt + 1) * 128, :], in_=y1_tm[qt]))(),
                deps=[t_y2], dma=True)

        with nc.Block() as block:
            def make_emitter(ename):
                def emit(eng):
                    wmax = {}
                    for fn, deps, inc, isdma in sch.ops[ename]:
                        for (de, dc) in deps:
                            if isdma or dc > wmax.get(de, 0):
                                eng.wait_ge(sems[de], dc)
                                wmax[de] = max(dc, wmax.get(de, 0))
                        ins = fn(eng)
                        ins.then_inc(sems[inc[0]], inc[1])
                    if ename == "Y":
                        for i in range(NLANES):
                            lane = f"Y{i}"
                            fin = sch.lane_cnt.get(lane, 0)
                            if fin:
                                eng.wait_ge(sems[lane], fin)
                return emit

            @block.tensor
            def _(t):
                make_emitter("T")(t)

            @block.vector
            def _(v):
                make_emitter("V")(v)

            @block.scalar
            def _(s):
                make_emitter("S")(s)

            @block.gpsimd
            def _(g):
                make_emitter("G")(g)

            @block.sync
            def _(y):
                make_emitter("Y")(y)

    return nc


_CACHED = {}


def _host_pack(inputs):
    q = np.asarray(inputs["L0_attn_query"], np.float32)
    Wqkv = np.asarray(inputs["L0_attn_Wqkv"], np.float32)
    Wproj = np.asarray(inputs["L0_attn_Wproj"], np.float32)
    bproj = np.asarray(inputs["L0_attn_bproj"], np.float32)
    Wr = np.asarray(inputs["L0_ffn_Wr"], np.float32)
    Wg = np.asarray(inputs["L0_ffn_Wg"], np.float32)
    Wu = np.asarray(inputs["L0_ffn_Wu"], np.float32)
    Wd = np.asarray(inputs["L0_ffn_Wd"], np.float32)
    Wg_s = np.asarray(inputs["L0_ffn_Wg_s"], np.float32)
    Wu_s = np.asarray(inputs["L0_ffn_Wu_s"], np.float32)
    Wd_s = np.asarray(inputs["L0_ffn_Wd_s"], np.float32)

    xT = np.ascontiguousarray(q.transpose(0, 2, 1))
    wp = np.ascontiguousarray(Wproj.T.reshape(NFC, 128, H))
    wr = np.ascontiguousarray(Wr.T.reshape(NFC, 128, 8).transpose(1, 0, 2).reshape(128, NFC * 8))
    Wg9 = np.concatenate([Wg_s[None], Wg], 0)
    Wu9 = np.concatenate([Wu_s[None], Wu], 0)
    Wd9 = np.concatenate([Wd_s[None], Wd], 0)
    wgp = np.ascontiguousarray(Wg9.reshape(9, NFC, 128, 8, 128).transpose(0, 3, 2, 1, 4))
    wup = np.ascontiguousarray(Wu9.reshape(9, NFC, 128, 8, 128).transpose(0, 3, 2, 1, 4))
    wdp = np.ascontiguousarray(Wd9.reshape(9, 8, 128, H))

    per_core = []
    for c in range(8):
        j, half = c // 2, c % 2
        s_lo = SH * half
        heads = [j, 4 + j, 8 + j, 12 + j]

        def pack_w(rows_off):
            mats = []
            for hd in heads:
                Wh = Wqkv[384 * hd + rows_off:384 * hd + rows_off + 128, :]
                mats.append(np.ascontiguousarray(Wh.T.reshape(NFC, 128, 128)))
            a = np.stack(mats, 0)
            return np.ascontiguousarray(a.transpose(2, 0, 1, 3).reshape(128, 4 * NFC * 128))
        per_core.append({
            "xT": xT,
            "xq": np.ascontiguousarray(xT[:, :, s_lo:s_lo + SH]),
            "xo": np.ascontiguousarray(q[j, s_lo:s_lo + SH, :]),
            "wq": pack_w(0),
            "wk": pack_w(128),
            "wv": pack_w(256),
            "wp": wp,
            "wr": wr,
            "ceye": np.eye(128, dtype=np.float32),
            "cone": np.ones((1, 128), np.float32),
            "cng2": np.full((128, 1), -2.0, np.float32),
            "bpj": bproj.reshape(1, H),
            "wgp": wgp,
            "wup": wup,
            "wdp": wdp,
        })
    return per_core


def kernel(**inputs):
    per_core = _host_pack(inputs)
    if "nc" not in _CACHED:
        _CACHED["nc"] = build_program()
    nc = _CACHED["nc"]
    res = run_bass_kernel_spmd(nc, per_core, core_ids=list(range(8)))
    out = np.empty((B, S, H), np.float32)
    for c in range(8):
        j, half = c // 2, c % 2
        out[j, SH * half:SH * (half + 1), :] = res.results[c]["out"]
    if DEBUG_TAPS:
        kernel.taps = [res.results[c] for c in range(8)]
    return out


# version 16 (final)
# speedup vs baseline: 1.0894x; 1.0020x over previous
import math
import numpy as np
from contextlib import ExitStack

import concourse.bass as bass
import concourse.mybir as mybir
from concourse.bass_utils import run_bass_kernel_spmd

F32 = mybir.dt.float32
BF16 = mybir.dt.bfloat16
AF = mybir.ActivationFunctionType
ALU = mybir.AluOpType
AX = mybir.AxisListType.X

B, S, H = 4, 2048, 2048
NH, HN = 16, 128
SH = 1024
NORM = 1.0 / math.sqrt(HN)
EPS = 1e-5
NFC = H // 128

DEBUG_TAPS = False
BISECT_NO_INTERLEAVE = False
BISECT_NO_LAG = False


NLANES = 8


class Sched:
    ENGS = ("T", "V", "S", "G", "Y")

    def __init__(self):
        self.ops = {e: [] for e in self.ENGS}
        self.cnt = {e: 0 for e in self.ENGS}
        self.lane_cnt = {}
        self.dma_idx = {"G": 0, "Y": 0}

    def add(self, e, fn, deps=(), dma=False):
        ds = [d for d in deps if d is not None]
        if dma and e in ("G", "Y"):
            lane = f"{e}{self.dma_idx[e] % NLANES}"
            self.dma_idx[e] += 1
            prev = self.lane_cnt.get(lane, 0)
            if prev:
                ds.append((lane, prev))
            self.lane_cnt[lane] = prev + 16
            self.ops[e].append((fn, ds, (lane, 16), True))
            return (lane, prev + 16)
        amt = 16 if dma else 1
        self.cnt[e] += amt
        tok = (e, self.cnt[e])
        self.ops[e].append((fn, ds, (e, amt), False))
        return tok


def build_program():
    nc = bass.Bass("TRN2", target_bir_lowering=False)

    xT = nc.dram_tensor("xT", [B, H, S], F32, kind="ExternalInput").ap()
    xq = nc.dram_tensor("xq", [B, H, SH], F32, kind="ExternalInput").ap()
    xo = nc.dram_tensor("xo", [SH, H], F32, kind="ExternalInput").ap()
    wq_d = nc.dram_tensor("wq", [128, 4 * NFC * 128], F32, kind="ExternalInput").ap()
    wk_d = nc.dram_tensor("wk", [128, 4 * NFC * 128], F32, kind="ExternalInput").ap()
    wv_d = nc.dram_tensor("wv", [128, 4 * NFC * 128], F32, kind="ExternalInput").ap()
    wp_d = nc.dram_tensor("wp", [NFC, 128, H], F32, kind="ExternalInput").ap()
    wr_d = nc.dram_tensor("wr", [128, NFC * 8], F32, kind="ExternalInput").ap()
    ceye = nc.dram_tensor("ceye", [128, 128], F32, kind="ExternalInput").ap()
    cone = nc.dram_tensor("cone", [1, 128], F32, kind="ExternalInput").ap()
    cng2 = nc.dram_tensor("cng2", [128, 1], F32, kind="ExternalInput").ap()
    bp_d = nc.dram_tensor("bpj", [1, H], F32, kind="ExternalInput").ap()
    wgp = nc.dram_tensor("wgp", [9, 8, 128, NFC, 128], F32, kind="ExternalInput").ap()
    wup = nc.dram_tensor("wup", [9, 8, 128, NFC, 128], F32, kind="ExternalInput").ap()
    wdp = nc.dram_tensor("wdp", [9, 8, 128, H], F32, kind="ExternalInput").ap()
    outp = nc.dram_tensor("out", [SH, H], F32, kind="ExternalOutput").ap()
    taps = {}
    if DEBUG_TAPS:
        taps["t_ctx"] = nc.dram_tensor("t_ctx", [128, 16 * 1024], BF16, kind="ExternalOutput").ap()
        taps["t_y1"] = nc.dram_tensor("t_y1", [SH, H], F32, kind="ExternalOutput").ap()
        taps["t_comb"] = nc.dram_tensor("t_comb", [8, 128, 8], F32, kind="ExternalOutput").ap()
        taps["t_kfm"] = nc.dram_tensor("t_kfm", [128, 2048], BF16, kind="ExternalOutput").ap()
        taps["t_vtm"] = nc.dram_tensor("t_vtm", [128, 2048], BF16, kind="ExternalOutput").ap()
        taps["t_qfm"] = nc.dram_tensor("t_qfm", [128, 1024], BF16, kind="ExternalOutput").ap()
        taps["t_pnt"] = nc.dram_tensor("t_pnt", [128, 4096], BF16, kind="ExternalOutput").ap()

    sch = Sched()
    A = sch.add

    with ExitStack() as ctx:
        e_ = ctx.enter_context

        POOL_BYTES = 212736
        pool = e_(nc.sbuf_tensor("pool", [128, POOL_BYTES // 2], BF16))

        def carve(off_b, shape, dtype):
            esz = 2 if dtype == BF16 else 4
            n = int(np.prod(shape[1:])) * esz
            assert off_b % 2 == 0 and off_b + n <= POOL_BYTES, (off_b, n)
            v = pool.ap()[:, off_b // 2:(off_b + n) // 2]
            if dtype != BF16:
                v = v.bitcast(dtype)
            if len(shape) == 3:
                v = v.rearrange("p (a b) -> p a b", a=shape[1], b=shape[2])
            elif len(shape) == 4:
                v = v.rearrange("p (a b c) -> p a b c", a=shape[1], b=shape[2], c=shape[3])
            if shape[0] != 128:
                v = v[0:shape[0]]
            return v

        o_consts = 0
        o_A0 = 6912
        o_A1 = 23296
        o_C = 39680
        o_K = 88832
        o_V = 121600
        o_Q = 154368
        o_VS = 162560
        o_PT = 163584
        o_PP = 171776
        o_CX = 179968
        o_BPR = 23296 + 8192
        assert o_CX + 32768 <= POOL_BYTES

        ident = carve(o_consts, [128, 128], BF16)
        wrp_sb = carve(o_consts + 2304, [128, NFC, 8], BF16)
        ones_bf = carve(o_consts + 2560, [1, 128], BF16)
        combT = carve(o_consts + 2816, [8, 1024], BF16)
        so = [o_consts + 4864]

        def small(n=1, dtype=F32):
            esz = 4 if dtype == F32 else 2
            o = so[0]
            so[0] += max(((n * esz + 7) // 8) * 8, 8)
            assert so[0] <= 6912
            return carve(o, [128, n], dtype)

        negmax = [small(), small()]
        bias_t = [small(), small()]
        neg2 = small()
        ell = [small(), small()]
        ellA = [small(), small()]
        ellB = [small(), small()]
        rell = [small(), small()]
        ln_negsum = small(); ln_negmean = small(); ln_ssq = small()
        ln_vs = small(); ln_sd = small(); ln_rstd = small()
        r_m1 = small(); r_m2 = small(); r_dd = small(); r_w1 = small(); r_w2 = small()
        r_d1 = small(8); r_i1 = small(8); r_l2 = small(8); r_d2 = small(8)
        r_i2 = small(8); r_t1 = small(8); r_comb = small(8); r_combbf = small(8, BF16)
        comb_flat = small(64)

        xslab = [carve(o_A0, [128, NFC, 512], BF16), carve(o_A1, [128, NFC, 512], BF16)]
        wq_sb = carve(o_C, [128, 4, NFC, 128], BF16)
        wk_sb = carve(o_C + 16384, [128, 4, NFC, 128], BF16)
        wv_sb = carve(o_C + 32768, [128, 4, NFC, 128], BF16)
        k_fm = [[carve(o_K + pb * 16384 + h * 4096, [128, 2048], BF16) for h in range(4)]
                for pb in range(2)]
        v_tm = [[carve(o_V + pb * 16384 + h * 4096, [128, 16, 128], BF16) for h in range(4)]
                for pb in range(2)]
        q_fm = [carve(o_Q + h * 2048, [128, 1024], BF16) for h in range(4)]
        vscr = carve(o_VS, [128, 512], BF16)
        bpr = carve(o_BPR, [1, 2048], BF16)
        PnT = carve(o_PT, [128, 16, 256], BF16)
        Pp = [carve(o_PP, [128, 2048], BF16), carve(o_PP + 4096, [128, 2048], BF16)]
        ctx_sb = carve(o_CX, [128, 16, 1024], BF16)

        x_own = carve(o_A0, [128, 2048], F32)
        wgu_slab = [carve(o_A0 + i * 4096, [128, NFC, 128], BF16) for i in range(4)]
        wp_sb = [carve(o_K, [128, NFC, 512], BF16), carve(o_K + 16384, [128, NFC, 512], BF16)]
        y1_fm = carve(o_K, [128, NFC, 1024], BF16)
        yc = carve(o_V, [128, 2048], F32)
        y1bf = carve(o_V + 8192, [128, 2048], BF16)
        h_sb = [carve(o_V + 12288, [128, 8, 512], BF16), carve(o_V + 20480, [128, 8, 512], BF16)]
        cb_sb = [carve(o_V + 28672, [128, 512], F32), carve(o_V + 30720, [128, 512], F32)]
        tg_s = carve(o_Q, [128, 512], F32)
        th_s = carve(o_Q + 2048, [128, 512], F32)
        wd_sb = carve(o_CX, [128, 8, 2048], BF16)
        y1_tm = [carve(o_C + i * 8192, [128, 2048], F32) for i in range(6)] + \
                [carve(o_PT, [128, 2048], F32), carve(o_A1, [128, 2048], F32)]

        PSM = e_(nc.psum_tensor("PSM", [128, 2048], F32))
        PSA = e_(nc.psum_tensor("PSA", [128, 512], F32))
        PSB = e_(nc.psum_tensor("PSB", [128, 512], F32))
        PTA = e_(nc.psum_tensor("PTA", [128, 1024], BF16))
        PSC = e_(nc.psum_tensor("PSC", [128, 512], F32))

        sems = {e: e_(nc.semaphore("sem" + e)) for e in Sched.ENGS}
        for e in ("G", "Y"):
            for i in range(NLANES):
                sems[f"{e}{i}"] = e_(nc.semaphore(f"sem{e}{i}"))

        t_ident = A("G", lambda g: g.dma_start(out=ident, in_=ceye), dma=True)
        t_ones = A("G", lambda g: g.dma_start(out=ones_bf, in_=cone), dma=True)
        t_neg2 = A("G", lambda g: g.dma_start(out=neg2, in_=cng2), dma=True)

        t_wq = A("G", lambda g: g.dma_start(out=wq_sb, in_=wq_d.rearrange(
            "p (a b c) -> p a b c", a=4, b=NFC, c=128)), dma=True)
        t_wk = A("G", lambda g: g.dma_start(out=wk_sb, in_=wk_d.rearrange(
            "p (a b c) -> p a b c", a=4, b=NFC, c=128)), dma=True)
        t_wv = A("G", lambda g: g.dma_start(out=wv_sb, in_=wv_d.rearrange(
            "p (a b c) -> p a b c", a=4, b=NFC, c=128)), dma=True)
        t_wr = A("G", lambda g: g.dma_start(out=wrp_sb, in_=wr_d.rearrange(
            "p (a b) -> p a b", a=NFC, b=8)), dma=True)
        t_bpr = None

        st = {
            "slab_last": [None, None],
            "psA": None, "psB": None, "ptA": None, "psC": None,
            "sl": 0,
            "slab_tok": None, "mm_tok": None, "vev_tok": None, "vscr_war": None,
            "qkv_last_T": None,
        }
        kvq_done = [None] * B
        grp_scores_last = {}
        grp_av_last = {}

        MMC = 4

        def make_kv_steps(bp):
            pb = bp % 2
            steps = []
            kwar = lambda r: grp_scores_last.get((bp - 2, r))
            vwar = lambda r: grp_av_last.get((bp - 2, r))

            def slab_load(src, sidx):
                def go():
                    st["slab_tok"] = A("G", (lambda: lambda g: g.dma_start(
                        out=xslab[sidx], in_=src))(), deps=[st["slab_last"][sidx]], dma=True)
                return go

            def mm_chunk(kind, hd, sidx, c0, wsb):
                def go(kind=kind, hd=hd, sidx=sidx, c0=c0, wsb=wsb):
                    ps = PSB if kind == "v" else PSA

                    def f(t):
                        for fc in range(c0, c0 + MMC):
                            ins = t.matmul(ps[:, 0:512], wsb[:, hd, fc, :],
                                           xslab[sidx][:, fc, :],
                                           start=(fc == 0), stop=(fc == NFC - 1))
                        return ins
                    deps = [st["slab_tok"], st["psB"] if kind == "v" else st["psA"]]
                    if c0 == 0:
                        deps.append(t_wv if kind == "v" else t_wk)
                    tok = A("T", f, deps=deps)
                    st["qkv_last_T"] = tok
                    st["mm_tok"] = tok
                    if c0 + MMC == NFC:
                        if kind == "v":
                            ev = A("V", (lambda: lambda v: v.tensor_copy(
                                vscr, PSB[:, 0:512]))(), deps=[tok, st["vscr_war"]])
                            st["psB"] = ev
                            st["vev_tok"] = ev
                        else:
                            tg = go.tg
                            ev = A("V", (lambda hd=hd, tg=tg: lambda v: v.tensor_copy(
                                k_fm[pb][hd][:, tg * 512:(tg + 1) * 512], PSA[:, 0:512]))(),
                                deps=[tok, kwar(hd) if tg == 0 else None])
                            st["psA"] = ev
                            kvq_done[bp] = ev
                return go

            def v_transpose(hd, tg):
                def go(hd=hd, tg=tg):
                    def f(t):
                        for tb in range(4):
                            ins = t.transpose(PTA[:, tb * 128:(tb + 1) * 128],
                                              vscr[:, tb * 128:(tb + 1) * 128], ident)
                        return ins
                    tok = A("T", f, deps=[st["vev_tok"], t_ident, st["ptA"]])
                    st["qkv_last_T"] = tok
                    st["vscr_war"] = tok
                    ev = A("V", (lambda hd=hd, tg=tg: lambda v: v.tensor_copy(
                        v_tm[pb][hd][:, tg * 4:tg * 4 + 4, :],
                        PTA[:, 0:512].rearrange("p (a b) -> p a b", a=4)))(),
                        deps=[tok, vwar(hd) if tg == 0 else None])
                    st["ptA"] = ev
                    kvq_done[bp] = ev
                return go

            for tg in range(4):
                sidx = st["sl"] % 2
                src = xT[bp].rearrange("(fc p) s -> p fc s", p=128)[:, :, tg * 512:(tg + 1) * 512]
                steps.append(slab_load(src, sidx))
                for hd in range(4):
                    for c0 in range(0, NFC, MMC):
                        g_ = mm_chunk("k", hd, sidx, c0, wk_sb)
                        g_.tg = tg
                        steps.append(g_)
                    for c0 in range(0, NFC, MMC):
                        g_ = mm_chunk("v", hd, sidx, c0, wv_sb)
                        g_.tg = tg
                        steps.append(g_)
                    steps.append(v_transpose(hd, tg))
                steps.append((lambda sidx=sidx: (lambda: st["slab_last"].__setitem__(
                    sidx, st["qkv_last_T"])))())
                st["sl"] += 1
            return steps

        def emit_q_block(bp):
            for qg in range(2):
                sidx = st["sl"] % 2
                src = xq[bp].rearrange("(fc p) s -> p fc s", p=128)[:, :, qg * 512:(qg + 1) * 512]
                st["slab_tok"] = A("G", (lambda src=src, sidx=sidx: lambda g: g.dma_start(
                    out=xslab[sidx], in_=src))(), deps=[st["slab_last"][sidx]], dma=True)
                for hd in range(4):
                    def f_q(t, hd=hd, sidx=sidx):
                        for fc in range(NFC):
                            ins = t.matmul(PSA[:, 0:512], wq_sb[:, hd, fc, :],
                                           xslab[sidx][:, fc, :],
                                           start=(fc == 0), stop=(fc == NFC - 1))
                        return ins
                    t_qmm = A("T", f_q, deps=[st["slab_tok"], t_wq, st["psA"],
                                              grp_scores_last.get((bp - 1, hd)) if qg == 0 else None])
                    st["qkv_last_T"] = t_qmm
                    t_qev = A("V", (lambda hd=hd, qg=qg: lambda v: v.tensor_copy(
                        q_fm[hd][:, qg * 512:(qg + 1) * 512], PSA[:, 0:512]))(), deps=[t_qmm])
                    st["psA"] = t_qev
                    kvq_done[bp] = t_qev
                st["slab_last"][sidx] = t_qmm
                st["sl"] += 1

        def drain(steps, n):
            for _ in range(min(n, len(steps))):
                steps.pop(0)()
            return steps

        tok_exp = {}
        tok_expA = {}
        tok_norm = {}
        tok_trans = {}
        tok_evacB = {}
        tok_av = {}
        ctx_evac = {}
        tok_exp_prev = None
        gqt = 0

        def emit_trans_av(g):
            bp_g = g // 32
            r_g = (g // 8) % 4
            qt_g = g % 8
            pb_g = bp_g % 2
            islot = 4 * bp_g + r_g
            par = g % 2
            pr = qt_g % 2

            def f_trA(t, par=par):
                for c in range(8):
                    ins = t.transpose(PTA[:, c * 128:(c + 1) * 128],
                                      Pp[par][:, c * 128:(c + 1) * 128], ident)
                return ins
            t_trA = A("T", f_trA, deps=[tok_norm[g], st["ptA"], t_ident])
            t_evA = A("V", (lambda pr=pr: lambda v: v.tensor_copy(
                PnT[:, 0:8, pr * 128:(pr + 1) * 128],
                PTA[:, 0:1024].rearrange("p (a b) -> p a b", a=8)))(),
                deps=[t_trA, tok_av.get(g // 2 - 1)])
            st["ptA"] = t_evA

            def f_trB(t, par=par):
                for c in range(8):
                    ins = t.transpose(PTA[:, c * 128:(c + 1) * 128],
                                      Pp[par][:, (8 + c) * 128:(9 + c) * 128], ident)
                return ins
            t_trB = A("T", f_trB, deps=[tok_norm[g], t_evA, t_ident])
            tok_trans[g] = t_trB
            t_evB = A("V", (lambda pr=pr: lambda v: v.tensor_copy(
                PnT[:, 8:16, pr * 128:(pr + 1) * 128],
                PTA[:, 0:1024].rearrange("p (a b) -> p a b", a=8)))(),
                deps=[t_trB, tok_av.get(g // 2 - 1)])
            st["ptA"] = t_evB
            tok_evacB[g] = t_evB

            if qt_g % 2 == 1:
                pair = g // 2

                def f_av(t, hd=r_g, pb=pb_g):
                    for kc in range(16):
                        ins = t.matmul(PSC[:, 0:256], v_tm[pb][hd][:, kc, :],
                                       PnT[:, kc, :], start=(kc == 0), stop=(kc == 15))
                    return ins
                t_avmm = A("T", f_av, deps=[tok_evacB[g - 1], t_evB, st["psC"]])
                tok_av[pair] = t_avmm
                grp_av_last[(bp_g, r_g)] = t_avmm
                pq = qt_g // 2
                t_cxe = A("V", (lambda islot=islot, pq=pq: lambda v: v.tensor_copy(
                    ctx_sb[:, islot, pq * 256:(pq + 1) * 256], PSC[:, 0:256]))(),
                    deps=[t_avmm])
                ctx_evac[pair] = t_cxe
                st["psC"] = t_cxe

        pend = make_kv_steps(0)
        while pend:
            pend = drain(pend, 1000)
        emit_q_block(0)
        if DEBUG_TAPS:
            A("Y", lambda y: y.dma_start(out=taps["t_kfm"], in_=k_fm[0][0]),
              deps=[kvq_done[0]], dma=True)
            A("Y", lambda y: y.dma_start(
                out=taps["t_vtm"], in_=v_tm[0][0].rearrange("p a b -> p (a b)")),
              deps=[kvq_done[0]], dma=True)
            A("Y", lambda y: y.dma_start(out=taps["t_qfm"], in_=q_fm[0]),
              deps=[kvq_done[0]], dma=True)

        for bp in range(B):
            pb = bp % 2
            nxt = make_kv_steps(bp + 1) if bp + 1 < B else []
            if BISECT_NO_INTERLEAVE:
                while nxt:
                    nxt = drain(nxt, 1000)
            drain_calls = [32]
            my_kvq = kvq_done[bp]

            for r in range(4):
                hd = r
                islot = 4 * bp + r
                for qt in range(8):
                    par = gqt % 2

                    def f_scA(t, hd=hd, qt=qt, pb=pb):
                        for fb in range(2):
                            ins = t.matmul(PSM[:, fb * 512:(fb + 1) * 512],
                                           q_fm[hd][:, qt * 128:(qt + 1) * 128],
                                           k_fm[pb][hd][:, fb * 512:(fb + 1) * 512],
                                           start=True, stop=True)
                        return ins
                    t_scA = A("T", f_scA, deps=[tok_expA.get(gqt - 1), my_kvq])

                    def f_scB(t, hd=hd, qt=qt, pb=pb):
                        for fb in range(2, 4):
                            ins = t.matmul(PSM[:, fb * 512:(fb + 1) * 512],
                                           q_fm[hd][:, qt * 128:(qt + 1) * 128],
                                           k_fm[pb][hd][:, fb * 512:(fb + 1) * 512],
                                           start=True, stop=True)
                        return ins
                    t_scB = A("T", f_scB, deps=[tok_exp_prev, my_kvq])
                    grp_scores_last[(bp, r)] = t_scB
                    t_mx = A("V", (lambda par=par: lambda v: v.reduce_max(
                        negmax[par], PSM[:, 0:256], axis=AX, negate=True))(),
                        deps=[t_scA, tok_exp.get(gqt - 2)])
                    t_bi = A("V", (lambda par=par: lambda v: v.tensor_scalar(
                        bias_t[par], negmax[par], NORM, -2.0, ALU.mult, ALU.add))(),
                        deps=[t_mx])
                    t_exA = A("S", (lambda par=par: lambda s: s.activation(
                        Pp[par][:, 0:1024], PSM[:, 0:1024], AF.Exp, bias=bias_t[par],
                        scale=NORM, accum_out=ellA[par]))(),
                        deps=[t_bi, tok_trans.get(gqt - 2), tok_norm.get(gqt - 2)])
                    tok_expA[gqt] = t_exA
                    t_exB = A("S", (lambda par=par: lambda s: s.activation(
                        Pp[par][:, 1024:2048], PSM[:, 1024:2048], AF.Exp,
                        bias=bias_t[par], scale=NORM, accum_out=ellB[par]))(),
                        deps=[t_scB])
                    tok_exp[gqt] = t_exB
                    tok_exp_prev = t_exB

                    nxt = drain(nxt, -(-len(nxt) // max(1, drain_calls[0])))
                    drain_calls[0] -= 1

                    if gqt >= 1:
                        emit_trans_av(gqt - 1)

                    t_el = A("V", (lambda par=par: lambda v: v.tensor_add(
                        ell[par], ellA[par], ellB[par]))(), deps=[t_exB])
                    t_rc = A("V", (lambda par=par: lambda v: v.reciprocal(
                        rell[par], ell[par]))(), deps=[t_el])
                    t_no = A("V", (lambda par=par: lambda v: v.tensor_scalar(
                        Pp[par], Pp[par], rell[par], None, ALU.mult))(), deps=[t_rc])
                    tok_norm[gqt] = t_no
                    gqt += 1

            if bp == B - 1 and not BISECT_NO_LAG:
                emit_trans_av(gqt - 1)
            while nxt:
                nxt = drain(nxt, 1000)
            if bp + 1 < B:
                emit_q_block(bp + 1)


        attn_done_V = ctx_evac[gqt // 2 - 1]
        attn_done_S = tok_exp[gqt - 1]
        attn_done_T = tok_av[gqt // 2 - 1]
        qkv_last_T = st["qkv_last_T"]
        last_scores_T = grp_scores_last[(B - 1, 3)]

        if DEBUG_TAPS:
            A("Y", lambda y: y.dma_start(out=taps["t_ctx"],
                                         in_=carve(o_CX, [128, 16 * 1024], BF16)),
              deps=[attn_done_V], dma=True)

        t_bpr = A("G", lambda g: g.dma_start(out=bpr, in_=bp_d),
                  deps=[qkv_last_T], dma=True)
        wp_buf_last = [grp_scores_last[(B - 2, 3)], grp_scores_last[(B - 1, 3)]]
        rs = {"psA": st["psA"], "psB": st["psB"], "ptA": st["ptA"], "psC": st["psC"],
              "psm": None,
              "y1bf_last": None, "sq_prev": None, "yc_war": None, "routerV": None,
              "combbf_last": None, "combT_done": None,
              "gu_buf_last": [None, None],
              "th1_prev": None, "h_last": [None, None], "h_tok": [None, None],
              "wd_reader": None, "wd_toks": [], "cb": [None, None]}
        y1_tok = [None] * 8
        y1fm_tok = [None] * 8
        y1acc = {}
        combT_evac = [None] * 8
        xb_tok = [None] * 8
        proj_evt = {}
        psm_slice_free = [attn_done_S] * 4
        xown_last_use = None
        proj_last_T = None

        def ln_stats(src_tile, deps_v, deps_s):
            t_ns = A("V", lambda v: v.reduce_sum(
                ln_negsum, src_tile, axis=AX, negate=True), deps=deps_v)
            t_sq = A("S", lambda s: s.activation(
                y1bf, src_tile, AF.Square, accum_out=ln_ssq),
                deps=deps_s + [rs["y1bf_last"], rs.get("ssq_reader")])
            rs["sq_prev"] = t_sq
            t_nm = A("V", lambda v: v.tensor_scalar(
                ln_negmean, ln_negsum, 1.0 / H, None, ALU.mult), deps=[t_ns])
            t_vs = A("V", lambda v: v.tensor_scalar(
                ln_vs, ln_ssq, 1.0 / H, EPS, ALU.mult, ALU.add), deps=[t_sq, t_nm])
            rs["ssq_reader"] = t_vs
            t_m2 = A("V", lambda v: v.tensor_tensor(
                ln_sd, ln_negmean, ln_negmean, ALU.mult), deps=[t_vs])
            t_va = A("V", lambda v: v.tensor_sub(ln_vs, ln_vs, ln_sd), deps=[t_m2])
            t_sd = A("S", lambda s: s.activation(ln_sd, ln_vs, AF.Sqrt), deps=[t_va])
            t_rs_ = A("V", lambda v: v.reciprocal(ln_rstd, ln_sd), deps=[t_sd])
            t_sh = A("V", lambda v: v.tensor_tensor(
                ln_negmean, ln_negmean, ln_rstd, ALU.mult), deps=[t_rs_])
            return t_sh

        def emit_ln1_chain(qt):
            t_sh = ln_stats(y1_tm[qt], [proj_evt[(qt, 3)], rs["yc_war"]],
                            [proj_evt[(qt, 3)]])
            t_y1 = A("V", (lambda qt=qt: lambda v: v.tensor_scalar(
                y1_tm[qt], y1_tm[qt], ln_rstd, ln_negmean, ALU.mult, ALU.add))(),
                deps=[t_sh])
            y1_tok[qt] = t_y1
            y1acc[qt] = t_y1
            rs["yc_war"] = t_y1

        def emit_ln1(qt):
            t_yb = A("V", (lambda qt=qt: lambda v: v.tensor_copy(
                y1bf, y1_tm[qt]))(), deps=[y1_tok[qt], rs["y1bf_last"]])

            def f_t1(t):
                for c in range(8):
                    ins = t.transpose(PTA[:, c * 128:(c + 1) * 128],
                                      y1bf[:, c * 128:(c + 1) * 128], ident)
                return ins
            t_t1 = A("T", f_t1, deps=[t_yb, rs["ptA"]])
            t_e1 = A("S", (lambda qt=qt: lambda s: s.copy(
                y1_fm[:, 0:8, qt * 128:(qt + 1) * 128],
                PTA[:, 0:1024].rearrange("p (a b) -> p a b", a=8)))(),
                deps=[t_t1, proj_last_T if qt == 0 else None])
            rs["ptA"] = t_e1

            def f_t2(t):
                for c in range(8):
                    ins = t.transpose(PTA[:, c * 128:(c + 1) * 128],
                                      y1bf[:, (8 + c) * 128:(9 + c) * 128], ident)
                return ins
            t_t2 = A("T", f_t2, deps=[t_yb, rs["ptA"]])
            rs["y1bf_last"] = t_t2
            t_e2 = A("S", (lambda qt=qt: lambda s: s.copy(
                y1_fm[:, 8:16, qt * 128:(qt + 1) * 128],
                PTA[:, 0:1024].rearrange("p (a b) -> p a b", a=8)))(), deps=[t_t2])
            rs["ptA"] = t_e2
            y1fm_tok[qt] = t_e2
            if DEBUG_TAPS:
                A("Y", (lambda qt=qt: lambda y: y.dma_start(
                    out=taps["t_y1"][qt * 128:(qt + 1) * 128, :], in_=y1_tm[qt]))(),
                  deps=[t_y1], dma=True)

        def emit_router(qt):
            def f_lg(t, qt=qt):
                for fc in range(NFC):
                    ins = t.matmul(PSC[:, 0:8], y1_fm[:, fc, qt * 128:(qt + 1) * 128],
                                   wrp_sb[:, fc, :], start=(fc == 0), stop=(fc == NFC - 1))
                return ins
            t_lg = A("T", f_lg, deps=[y1fm_tok[qt], t_wr, rs["psC"]])
            t1 = A("V", lambda v: v.reduce_max(r_m1, PSC[:, 0:8], axis=AX),
                   deps=[t_lg, rs["routerV"]])
            t2 = A("V", lambda v: v.tensor_scalar(r_d1, PSC[:, 0:8], r_m1, None, ALU.subtract),
                   deps=[t1])
            t3 = A("V", lambda v: v.tensor_scalar(r_i1, r_d1, 0.0, None, ALU.is_ge), deps=[t2])
            t4 = A("V", lambda v: v.scalar_tensor_tensor(r_l2, r_i1, -1e9, PSC[:, 0:8],
                                                         ALU.mult, ALU.add), deps=[t3])
            t5 = A("V", lambda v: v.reduce_max(r_m2, r_l2, axis=AX), deps=[t4])
            t6 = A("V", lambda v: v.tensor_scalar(r_d2, r_l2, r_m2, None, ALU.subtract), deps=[t5])
            t7 = A("V", lambda v: v.tensor_scalar(r_i2, r_d2, 0.0, None, ALU.is_ge), deps=[t6])
            t8 = A("V", lambda v: v.tensor_sub(r_dd, r_m2, r_m1), deps=[t7])
            t9 = A("S", lambda s: s.activation(r_w1, r_dd, AF.Sigmoid, scale=-1.0), deps=[t8])
            t10 = A("V", lambda v: v.tensor_scalar(r_w2, r_w1, -1.0, 1.0, ALU.mult, ALU.add),
                    deps=[t9])
            t11 = A("V", lambda v: v.tensor_scalar(r_t1, r_i1, r_w1, None, ALU.mult), deps=[t10])
            t12 = A("V", lambda v: v.scalar_tensor_tensor(r_comb, r_i2, r_w2, r_t1,
                                                          ALU.mult, ALU.add), deps=[t11])
            rs["psC"] = t12
            t13 = A("V", (lambda qt=qt: lambda v: v.tensor_copy(
                comb_flat[:, qt * 8:(qt + 1) * 8], r_comb))(), deps=[t12])
            rs["routerV"] = t13
            combT_evac[qt] = t13
            if DEBUG_TAPS:
                A("Y", (lambda qt=qt: lambda y: y.dma_start(
                    out=taps["t_comb"][qt], in_=r_comb))(), deps=[t12], dma=True)

        for qp in range(4):
            bidx = qp % 2
            t_wp = A("G", (lambda qp=qp, bidx=bidx: lambda g: g.dma_start(
                out=wp_sb[bidx], in_=wp_d[:, :, qp * 512:(qp + 1) * 512].rearrange(
                    "a p s -> p a s")))(), deps=[wp_buf_last[bidx]], dma=True)
            t_pj_last = None
            for qt in range(8):
                if qp == 0:
                    t_xo = A("Y", (lambda qt=qt: lambda y: y.dma_start(
                        out=y1_tm[qt], in_=xo[qt * 128:(qt + 1) * 128, :]))(),
                        deps=[qkv_last_T if qt == 0 else None,
                              attn_done_T if qt == 0 else None,
                              attn_done_S if qt == 0 else None], dma=True)
                    xb_tok[qt] = t_xo
                    xown_last_use = t_xo
                sli = qt % 4

                def f_pj(t, qt=qt, bidx=bidx, sli=sli, qp=qp):
                    for i in range(16):
                        t.matmul(PSM[:, sli * 512:(sli + 1) * 512],
                                 ctx_sb[:, i, qt * 128:(qt + 1) * 128],
                                 wp_sb[bidx][:, i, :],
                                 start=(i == 0), stop=False)
                    return t.matmul(PSM[:, sli * 512:(sli + 1) * 512], ones_bf,
                                    bpr[:, qp * 512:(qp + 1) * 512],
                                    start=False, stop=True)
                t_pj = A("T", f_pj, deps=[t_wp, attn_done_V, psm_slice_free[sli],
                                          t_ones, t_bpr])
                t_pj_last = t_pj
                proj_last_T = t_pj
                t_ev = A("V", (lambda qt=qt, qp=qp, sli=sli: lambda v: v.tensor_tensor(
                    y1_tm[qt][:, qp * 512:(qp + 1) * 512], PSM[:, sli * 512:(sli + 1) * 512],
                    y1_tm[qt][:, qp * 512:(qp + 1) * 512], ALU.add))(),
                    deps=[t_pj, xb_tok[qt]])
                proj_evt[(qt, qp)] = t_ev
                psm_slice_free[sli] = t_ev
                if qp == 3:
                    emit_ln1_chain(qt)
            wp_buf_last[bidx] = t_pj_last


        y1fm_grp = {0: lambda: y1fm_tok[3], 1: lambda: y1fm_tok[7]}

        def emit_gu_block(ei, grps, load_wd, its=tuple(range(8))):
            if load_wd:
                rs["_wd_pend"] = ei
                rs["_wd_cnt"] = 0
                rs["wd_toks"] = []
            for it in its:
                gb = it % 2
                t_wg = A("G", (lambda ei=ei, it=it, gb=gb: lambda g: g.dma_start(
                    out=wgu_slab[gb], in_=wgp[ei, it]))(), deps=[rs["gu_buf_last"][gb]],
                    dma=True)
                t_wu = A("G", (lambda ei=ei, it=it, gb=gb: lambda g: g.dma_start(
                    out=wgu_slab[2 + gb], in_=wup[ei, it]))(), deps=[rs["gu_buf_last"][gb]],
                    dma=True)
                if rs.get("_wd_pend") is not None:
                    rs["_wd_cnt"] += 1
                    cnt = rs["_wd_cnt"]
                    wei = rs["_wd_pend"]
                    if cnt <= 8:
                        rs["wd_toks"].append(A("G", (lambda wei=wei, ic=cnt - 1:
                                                     lambda g:
                                              g.dma_start(out=wd_sb[:, ic, :],
                                                          in_=wdp[wei, ic]))(),
                            deps=[rs["wd_reader"],
                                  proj_last_T if wei == 0 else None],
                            dma=True))
                    if cnt == 8 or (it == its[-1] and ei != 0 and cnt < 8):
                        while rs["_wd_cnt"] < 8:
                            rs["_wd_cnt"] += 1
                            ic = rs["_wd_cnt"] - 1
                            rs["wd_toks"].append(A("G", (lambda wei=wei, ic=ic:
                                                         lambda g:
                                                  g.dma_start(out=wd_sb[:, ic, :],
                                                              in_=wdp[wei, ic]))(),
                                deps=[rs["wd_reader"],
                                      proj_last_T if wei == 0 else None], dma=True))
                        rs["_wd_pend"] = None
                last_u = None
                for grp in grps:
                    def f_g(t, gb=gb, grp=grp):
                        for fc in range(NFC):
                            ins = t.matmul(PSA[:, 0:512], wgu_slab[gb][:, fc, :],
                                           y1_fm[:, fc, grp * 512:(grp + 1) * 512],
                                           start=(fc == 0), stop=(fc == NFC - 1))
                        return ins
                    t_gmm = A("T", f_g, deps=[t_wg, y1fm_grp[grp](), rs["psA"]])

                    def f_u(t, gb=gb, grp=grp):
                        for fc in range(NFC):
                            ins = t.matmul(PSB[:, 0:512], wgu_slab[2 + gb][:, fc, :],
                                           y1_fm[:, fc, grp * 512:(grp + 1) * 512],
                                           start=(fc == 0), stop=(fc == NFC - 1))
                        return ins
                    t_umm = A("T", f_u, deps=[t_wu, y1fm_grp[grp](), rs["psB"]])
                    last_u = t_umm
                    t_si = A("S", lambda s: s.activation(tg_s, PSA[:, 0:512], AF.Silu),
                             deps=[t_gmm, rs["th1_prev"],
                                   last_scores_T if (ei == 0 and it == 0) else None])
                    rs["psA"] = t_si
                    t_h = A("V", (lambda it=it, grp=grp: lambda v: v.tensor_tensor(
                        h_sb[grp][:, it, :], tg_s, PSB[:, 0:512], ALU.mult))(),
                        deps=[t_si, t_umm, rs["h_last"][grp],
                              attn_done_T if (ei == 0 and it == 0) else None])
                    rs["psB"] = t_h
                    rs["th1_prev"] = t_h
                    rs["h_tok"][grp] = t_h
                rs["gu_buf_last"][gb] = last_u

        def emit_wd(ei, grp):
            wd_mm_last = None

            for tt in range(4):
                qt = grp * 4 + tt
                for fb in range(4):
                    def f_wd(t, grp=grp, tt=tt, fb=fb):
                        for ic in range(8):
                            ins = t.matmul(PSM[:, fb * 512:(fb + 1) * 512],
                                           h_sb[grp][:, ic, tt * 128:(tt + 1) * 128],
                                           wd_sb[:, ic, fb * 512:(fb + 1) * 512],
                                           start=(ic == 0), stop=(ic == 7))
                        return ins
                    t_wdm = A("T", f_wd, deps=[rs["h_tok"][grp], rs["psm"][fb]]
                              + rs["wd_toks"])
                    wd_mm_last = t_wdm
                    if ei == 0:
                        t_ac = A("V", (lambda qt=qt, fb=fb: lambda v: v.tensor_tensor(
                            y1_tm[qt][:, fb * 512:(fb + 1) * 512],
                            PSM[:, fb * 512:(fb + 1) * 512],
                            y1_tm[qt][:, fb * 512:(fb + 1) * 512], ALU.add))(),
                            deps=[t_wdm, y1acc[qt]])
                    else:
                        ci = qt * 8 + (ei - 1)
                        t_ac = A("V", (lambda qt=qt, fb=fb, ci=ci: lambda v:
                                       v.scalar_tensor_tensor(
                            y1_tm[qt][:, fb * 512:(fb + 1) * 512],
                            PSM[:, fb * 512:(fb + 1) * 512],
                            comb_flat[:, ci:ci + 1],
                            y1_tm[qt][:, fb * 512:(fb + 1) * 512],
                            ALU.mult, ALU.add))(),
                            deps=[t_wdm, y1acc[qt], combT_evac[qt]])
                    y1acc[qt] = t_ac
                    rs["psm"][fb] = t_ac
            rs["h_last"][grp] = wd_mm_last
            rs["wd_reader"] = wd_mm_last

        for qt in range(4):
            emit_ln1(qt)
            emit_router(qt)
        rs["psm"] = list(psm_slice_free)
        rs["gu_buf_last"] = [xown_last_use, xown_last_use]
        first_gu = [True]
        for qt in range(4, 8):
            its = {4: (0, 1), 5: (2, 3), 6: (4, 5), 7: (6, 7)}[qt]
            emit_gu_block(0, [0], load_wd=first_gu[0], its=its)
            first_gu[0] = False
            emit_ln1(qt)
            emit_router(qt)
        emit_wd(0, 0)
        emit_gu_block(0, [1], load_wd=False)
        emit_wd(0, 1)
        for ei in range(1, 9):
            emit_gu_block(ei, [0, 1], load_wd=True)
            emit_wd(ei, 0)
            emit_wd(ei, 1)

        gu_buf_last = rs["gu_buf_last"]
        wd_last_reader = rs["wd_reader"]

        ybf_free = list(gu_buf_last)
        for qt in range(8):
            extra = (ybf_free + [wd_last_reader]) if qt == 0 else []
            t_sh = ln_stats(y1_tm[qt], [y1acc[qt]], [y1acc[qt]] + extra)
            t_y2 = A("V", (lambda qt=qt: lambda v: v.tensor_scalar(
                y1_tm[qt], y1_tm[qt], ln_rstd, ln_negmean, ALU.mult, ALU.add))(),
                deps=[t_sh])
            A("Y", (lambda qt=qt: lambda y: y.dma_start(
                out=outp[qt * 128:(qt + 1) * 128, :], in_=y1_tm[qt]))(),
                deps=[t_y2], dma=True)

        with nc.Block() as block:
            def make_emitter(ename):
                def emit(eng):
                    wmax = {}
                    for fn, deps, inc, isdma in sch.ops[ename]:
                        for (de, dc) in deps:
                            if isdma or dc > wmax.get(de, 0):
                                eng.wait_ge(sems[de], dc)
                                wmax[de] = max(dc, wmax.get(de, 0))
                        ins = fn(eng)
                        ins.then_inc(sems[inc[0]], inc[1])
                    if ename == "Y":
                        for i in range(NLANES):
                            lane = f"Y{i}"
                            fin = sch.lane_cnt.get(lane, 0)
                            if fin:
                                eng.wait_ge(sems[lane], fin)
                return emit

            @block.tensor
            def _(t):
                make_emitter("T")(t)

            @block.vector
            def _(v):
                make_emitter("V")(v)

            @block.scalar
            def _(s):
                make_emitter("S")(s)

            @block.gpsimd
            def _(g):
                make_emitter("G")(g)

            @block.sync
            def _(y):
                make_emitter("Y")(y)

    return nc


_CACHED = {}


def _host_pack(inputs):
    q = np.asarray(inputs["L0_attn_query"], np.float32)
    Wqkv = np.asarray(inputs["L0_attn_Wqkv"], np.float32)
    Wproj = np.asarray(inputs["L0_attn_Wproj"], np.float32)
    bproj = np.asarray(inputs["L0_attn_bproj"], np.float32)
    Wr = np.asarray(inputs["L0_ffn_Wr"], np.float32)
    Wg = np.asarray(inputs["L0_ffn_Wg"], np.float32)
    Wu = np.asarray(inputs["L0_ffn_Wu"], np.float32)
    Wd = np.asarray(inputs["L0_ffn_Wd"], np.float32)
    Wg_s = np.asarray(inputs["L0_ffn_Wg_s"], np.float32)
    Wu_s = np.asarray(inputs["L0_ffn_Wu_s"], np.float32)
    Wd_s = np.asarray(inputs["L0_ffn_Wd_s"], np.float32)

    xT = np.ascontiguousarray(q.transpose(0, 2, 1))
    wp = np.ascontiguousarray(Wproj.T.reshape(NFC, 128, H))
    wr = np.ascontiguousarray(Wr.T.reshape(NFC, 128, 8).transpose(1, 0, 2).reshape(128, NFC * 8))
    Wg9 = np.concatenate([Wg_s[None], Wg], 0)
    Wu9 = np.concatenate([Wu_s[None], Wu], 0)
    Wd9 = np.concatenate([Wd_s[None], Wd], 0)
    wgp = np.ascontiguousarray(Wg9.reshape(9, NFC, 128, 8, 128).transpose(0, 3, 2, 1, 4))
    wup = np.ascontiguousarray(Wu9.reshape(9, NFC, 128, 8, 128).transpose(0, 3, 2, 1, 4))
    wdp = np.ascontiguousarray(Wd9.reshape(9, 8, 128, H))

    per_core = []
    for c in range(8):
        j, half = c // 2, c % 2
        s_lo = SH * half
        heads = [j, 4 + j, 8 + j, 12 + j]

        def pack_w(rows_off):
            mats = []
            for hd in heads:
                Wh = Wqkv[384 * hd + rows_off:384 * hd + rows_off + 128, :]
                mats.append(np.ascontiguousarray(Wh.T.reshape(NFC, 128, 128)))
            a = np.stack(mats, 0)
            return np.ascontiguousarray(a.transpose(2, 0, 1, 3).reshape(128, 4 * NFC * 128))
        per_core.append({
            "xT": xT,
            "xq": np.ascontiguousarray(xT[:, :, s_lo:s_lo + SH]),
            "xo": np.ascontiguousarray(q[j, s_lo:s_lo + SH, :]),
            "wq": pack_w(0),
            "wk": pack_w(128),
            "wv": pack_w(256),
            "wp": wp,
            "wr": wr,
            "ceye": np.eye(128, dtype=np.float32),
            "cone": np.ones((1, 128), np.float32),
            "cng2": np.full((128, 1), -2.0, np.float32),
            "bpj": bproj.reshape(1, H),
            "wgp": wgp,
            "wup": wup,
            "wdp": wdp,
        })
    return per_core


def kernel(**inputs):
    per_core = _host_pack(inputs)
    if "nc" not in _CACHED:
        _CACHED["nc"] = build_program()
    nc = _CACHED["nc"]
    res = run_bass_kernel_spmd(nc, per_core, core_ids=list(range(8)))
    out = np.empty((B, S, H), np.float32)
    for c in range(8):
        j, half = c // 2, c % 2
        out[j, SH * half:SH * (half + 1), :] = res.results[c]["out"]
    if DEBUG_TAPS:
        kernel.taps = [res.results[c] for c in range(8)]
    return out
